# version 11
# baseline (speedup 1.0000x reference)
import math
import contextlib
import numpy as np
import concourse.bass as bass
import concourse.mybir as mybir

F32 = mybir.dt.float32
BF16 = mybir.dt.bfloat16
AF = mybir.ActivationFunctionType
ALU = mybir.AluOpType
AX = mybir.AxisListType

P = 128
D = 2048
T = 8192
NH = 16
CH = 144
QL = 2048
NKC = 16
NQC = 4
FQ, FKV = 864, 576
FQM = [128] * 6 + [96]
FKVM = [128] * 4 + [64]
HID = 5461
NFC = 43
FC_LAST = HID - 128 * (NFC - 1)
SCALE = float(1.0 / math.sqrt(128.0))
EXPB = -12.0
EPS = 1e-5
NW = HID * D
f32 = np.float32


def cdiv(a, b):
    return (a + b - 1) // b


class Op:
    __slots__ = ("eng", "build", "waits", "sem", "val", "k")


class MK:
    SERIAL = ("vector", "scalar", "gpsimd")

    def __init__(self, dbg=None, phases=("p0", "p1", "p2", "p3", "p4", "p56")):
        self.dbg = dbg
        self.phases = phases
        nc = self.nc = bass.Bass("TRN2", target_bir_lowering=False)
        dr = nc.dram_tensor
        self.xk = dr("xk", [D, T], F32, kind="ExternalInput")
        self.xqd = dr("xq", [QL, D], F32, kind="ExternalInput")
        self.xqT = dr("xqT", [D, QL], F32, kind="ExternalInput")
        self.wqd = dr("wq", [D, FQ], F32, kind="ExternalInput")
        self.wkvd = dr("wkv", [D, FKV], F32, kind="ExternalInput")
        self.wod = dr("wo", [D, D], F32, kind="ExternalInput")
        self.w1t = dr("w1t", [D, HID], F32, kind="ExternalInput")
        self.w3t = dr("w3t", [D, HID], F32, kind="ExternalInput")
        self.w2t = dr("w2t", [HID, D], F32, kind="ExternalInput")
        self.cosb = dr("cosb", [T, 64], F32, kind="ExternalInput")
        self.sinb = dr("sinb", [T, 64], F32, kind="ExternalInput")
        self.dmaskd = dr("dmask", [16, 128, 512], BF16, kind="ExternalInput")
        self.identd = dr("ident", [P, P], F32, kind="ExternalInput")
        self.yd = dr("y", [QL, D], F32, kind="ExternalOutput")
        self.ksc = dr("ksc", [T, D], BF16, kind="Internal")
        self.vsc = dr("vsc", [T, D], BF16, kind="Internal")
        self.qsc = dr("qsc", [QL, D], BF16, kind="Internal")
        self.x2sc = dr("x2sc", [QL, D], F32, kind="Internal")
        self.ln3sc = dr("ln3sc", [D, QL], BF16, kind="Internal")
        self.wq1sc = dr("wq1sc", [NFC, P, 2048], BF16, kind="Internal")
        self.wq3sc = dr("wq3sc", [NFC, P, 2048], BF16, kind="Internal")
        self.wq2sc = dr("wq2sc", [16, P, NFC * 128], BF16, kind="Internal")
        if dbg == "kv":
            self.okv_k = dr("okv_k", [T, D], BF16, kind="ExternalOutput")
            self.okv_v = dr("okv_v", [T, D], BF16, kind="ExternalOutput")
        if dbg == "q":
            self.oq = dr("oq", [QL, D], BF16, kind="ExternalOutput")
        if dbg == "attn":
            self.oattn = dr("oattn", [NH * P, QL], BF16, kind="ExternalOutput")
        if dbg == "x2":
            self.ox2 = dr("ox2", [QL, D], F32, kind="ExternalOutput")

        self.sems = {}
        for name in ("sync", "vector", "tensor", "scalar", "gpsimd"):
            h = nc.semaphore("s_" + name).__enter__()
            self.sems[name] = [h, 0]
        self.marks = {e: {} for e in ("sync", "vector", "tensor", "scalar", "gpsimd")}

        A = nc.alloc_sbuf_tensor
        self.ident_s = A("ident_s", [P, P], F32)
        self.ones_c = A("ones_c", [P, 1], BF16)
        self.nones_c = A("nones_c", [P, 1], BF16)
        self.ones_rf = A("ones_rf", [1, P], F32)
        self.ones_cf = A("ones_cf", [P, 1], F32)
        self.scal = A("scal", [P, 8], F32)
        self.parts = A("parts", [P, 224], F32)

    def new_phase(self):
        self.ops = []
        self.last_on = {}

    def add(self, eng, build, waits=(), k=None, dma=None):
        op = Op()
        op.eng = eng
        op.build = build
        ws = [w for w in waits if w is not None]
        if eng in self.SERIAL and self.last_on.get(eng) is not None:
            ws.append(self.last_on[eng])
        op.waits = ws
        if dma:
            sem_name = "dma_" + dma
            if sem_name not in self.sems:
                h = self.nc.semaphore("s_" + sem_name).__enter__()
                self.sems[sem_name] = [h, 0]
            op.k = 16 if k is None else k
        else:
            sem_name = eng
            op.k = 1 if k is None else k
        s = self.sems[sem_name]
        s[1] += op.k
        op.sem = sem_name
        op.val = s[1]
        self.ops.append(op)
        self.last_on[eng] = op
        return op

    def emit(self, name):
        nc = self.nc
        by_eng = {}
        for op in self.ops:
            by_eng.setdefault(op.eng, []).append(op)
        with nc.Block(name) as blk:
            for eng_name, ops in by_eng.items():
                def body(eng, ops=ops, eng_name=eng_name):
                    marks = self.marks[eng_name]
                    for op in ops:
                        for w in op.waits:
                            if marks.get(w.sem, 0) < w.val:
                                eng.wait_ge(self.sems[w.sem][0], w.val)
                                marks[w.sem] = w.val
                        instr = op.build(eng)
                        instr.then_inc(self.sems[op.sem][0], op.k)
                getattr(blk, eng_name)(body)
            def drainer(eng):
                marks = self.marks["sync"]
                for sn, (h, n) in self.sems.items():
                    if n > 0 and marks.get(sn, 0) < n:
                        eng.wait_ge(h, n)
                        marks[sn] = n
            blk.sync(drainer)
        for e, marks in self.marks.items():
            for sn, (h, n) in self.sems.items():
                marks[sn] = max(marks.get(sn, 0), n)
        self.ops = None

    def build(self):
        if "p0" in self.phases:
            self.p0()
        if "p1" in self.phases:
            self.p1()
        if "p2" in self.phases:
            self.p2()
        if self.dbg in ("kv", "q"):
            self.dbg_dump()
            return self.nc
        if "p3" in self.phases:
            self.p3()
            if self.dbg == "attn":
                return self.nc
        if "p4" in self.phases:
            self.p4()
            if self.dbg == "x2":
                return self.nc
        if "p56" in self.phases:
            self.p56()
        return self.nc

    def p0(self):
        nc = self.nc
        with contextlib.ExitStack() as stk:

            self.new_phase()
            add = self.add
            o_id = add("sync", lambda e: e.dma_start(self.ident_s[:, :], self.identd[:, :]), dma="id")

            consts = [
                add("vector", lambda v: v.memset(self.ones_c[:, :], 1.0)),
                add("vector", lambda v: v.memset(self.nones_c[:, :], -1.0)),
                add("vector", lambda v: v.memset(self.ones_rf[:, :], 1.0)),
                add("vector", lambda v: v.memset(self.ones_cf[:, :], 1.0)),
            ]
            self.emit("p0")


    def _wrows(self, add, wbf, wdram, fdim, dst, ps_ws):
        o_w = add("gpsimd", lambda g: g.dma_start(
            wbf[:, :, :], wdram.ap().rearrange("(m p) f -> p m f", p=P)), dma="wrow")
        prev_ev = None
        for c0 in range(0, fdim, 512):
            cw = min(512, fdim - c0)
            def mmws(t, c0=c0, cw=cw):
                for m in range(16):
                    ins = t.matmul(ps_ws[0:1, 0:cw], self.ones_c[:, :],
                                   wbf[:, m, c0:c0 + cw], start=(m == 0), stop=(m == 15))
                return ins
            o_mm = add("tensor", mmws, waits=[o_w, prev_ev])
            prev_ev = add("vector", lambda v, c0=c0, cw=cw: v.tensor_scalar_mul(
                dst[0:1, c0:c0 + cw], ps_ws[0:1, 0:cw], float(1.0 / D)), waits=[o_mm])
        return o_w, prev_ev
    def _stats_chunk(self, add, ctx, xbf_v, xsq_v, deps_x, deps_sq,
                     war_st, war_sm, war_negrow, war_strip):
        strip0, strip1, negrow, stt, r2h, ps_st, ps_sm = (
            ctx["strip0"], ctx["strip1"], ctx["negrow"], ctx["stt"], ctx["r2h"],
            ctx["ps_st"], ctx["ps_sm"])
        def mm_neg(t):
            for m in range(16):
                ins = t.matmul(ps_st[0:1, :], self.nones_c[:, :], xbf_v[:, m, :],
                               start=(m == 0), stop=(m == 15))
            return ins
        o_neg = add("tensor", mm_neg, waits=deps_x + war_st)
        def mm_s2(t):
            for m in range(16):
                ins = t.matmul(ps_st[32:33, :], self.ones_c[:, :], xsq_v[:, m, :],
                               start=(m == 0), stop=(m == 15))
            return ins
        o_s2 = add("tensor", mm_s2, waits=deps_sq + war_st)
        o_st0 = add("vector", lambda v: v.tensor_copy(strip0[0:1, :], ps_st[0:1, :]),
                    waits=[o_neg] + war_strip)
        o_st1 = add("vector", lambda v: v.tensor_copy(strip1[0:1, :], ps_st[32:33, :]), waits=[o_s2])
        o_nr = add("vector", lambda v: v.tensor_copy(negrow[0:1, :], ps_st[0:1, :]),
                   waits=war_negrow)
        tts = []
        for tt in range(4):
            tts.append(add("tensor", lambda t, tt=tt: t.transpose(
                ps_sm[:, tt * 2:tt * 2 + 1],
                strip0[0:1, tt * 128:(tt + 1) * 128], self.ident_s[0:1, 0:1]),
                waits=[o_st0, o_st1] + war_sm))
            tts.append(add("tensor", lambda t, tt=tt: t.transpose(
                ps_sm[:, tt * 2 + 1:tt * 2 + 2],
                strip1[0:1, tt * 128:(tt + 1) * 128], self.ident_s[0:1, 0:1])))
        sview = ps_sm[:, 0:8].rearrange("p (tt two) -> p tt two", two=2)
        negc = sview[:, :, 0:1].squeeze(2)
        s2c = sview[:, :, 1:2].squeeze(2)
        o_c1 = add("vector", lambda v: v.tensor_copy(stt[:, 4:8], negc), waits=tts)
        o_c2 = add("vector", lambda v: v.tensor_scalar(
            stt[:, 0:4], s2c, float(1.0 / D), float(EPS), ALU.mult, ALU.add))
        add("vector", lambda v: v.tensor_mul(stt[:, 4:8], stt[:, 4:8], stt[:, 4:8]))
        add("vector", lambda v: v.scalar_tensor_tensor(
            stt[:, 0:4], stt[:, 4:8], float(-1.0 / D / D), stt[:, 0:4], ALU.mult, ALU.add))
        add("vector", lambda v: v.reciprocal(stt[:, 4:8], stt[:, 0:4]))
        o_r2 = add("vector", lambda v: v.tensor_scalar_mul(r2h[:, 0:4], stt[:, 4:8], float(0.5)))
        return o_nr, o_r2, (o_neg, o_s2)

    def _proj_build(self, add, hist, c, xbf, wbf, wsrow, negrow, ps_ab, abkv_f,
                    Ms, nM, o_castlast, o_nr, war_form):
        projs = []
        evs = []
        allev = hist[c - 1]["abev_all"] if c >= 1 else []
        for mc in range(nM):
            mw = Ms[mc]
            c0 = mc * 128
            wws = [o_castlast, o_nr]
            gi = c * nM + mc
            if gi >= 2:
                if mc >= 2:
                    wws.append(evs[mc - 2])
                else:
                    wws.append(allev[nM - 2 + mc])
            def mmproj(t, c=c, mc=mc, mw=mw, c0=c0, gi=gi):
                for m in range(16):
                    t.matmul(ps_ab[0:mw, gi % 2, :], wbf[:, m, c0:c0 + mw],
                             xbf[:, c % 2, m, :], start=(m == 0), stop=False)
                return t.matmul(ps_ab[0:mw, gi % 2, :], wsrow[0:1, c0:c0 + mw],
                                negrow[0:1, :], start=False, stop=True)
            projs.append(add("tensor", mmproj, waits=wws))
            evs.append(add("scalar", lambda e, c=c, mc=mc, mw=mw, gi=gi: e.copy(
                abkv_f[0:mw, c % 2, mc, :], ps_ab[0:mw, gi % 2, :]),
                waits=[projs[mc]] + war_form))
        return projs, evs

    def p1(self):
        nc = self.nc
        with contextlib.ExitStack() as stk:
            E = lambda *a: stk.enter_context(nc.sbuf_tensor(*a))
            wkv_bf = E("wkv_bf", [P, 16, FKV], BF16)
            wskv_r = E("wskv_r", [1, FKV], BF16)
            ps_ws = stk.enter_context(nc.psum_tensor("ws_ps1", [1, 512], F32))
            xstage = E("xstage", [P, 2, 16, 128], F32)
            xbf = E("xbf", [P, 2, 16, 512], BF16)
            xsq = E("xsq", [P, 16, 512], BF16)
            strip0 = E("strip0", [1, 512], F32)
            strip1 = E("strip1", [1, 512], F32)
            negrow = E("negrow", [1, 512], BF16)
            stt = E("stt", [P, 8], F32)
            r2h = E("r2h", [P, 4], F32)
            abkv_f = E("abkv_f", [P, 2, 5, 512], F32)
            abkv_tm = E("abkv_tm", [P, 2, 4, 576], F32)
            rott = E("rott", [P, 4, 4, 64], F32)
            costile = E("costile", [P, 2, 4, 64], F32)
            sintile = E("sintile", [P, 2, 4, 64], F32)
            ftmp1 = E("ftmp1", [P, 2048], F32)
            ftmp2 = E("ftmp2", [P, 2048], F32)
            vtmp = E("vtmp", [P, 2048], BF16)
            k_tm = E("k_tm", [P, 4, 2048], BF16)
            v_tm = E("v_tm", [P, 4, 2048], BF16)
            ps_st = stk.enter_context(nc.psum_tensor("ps_st", [P, 512], F32))
            ps_sm = stk.enter_context(nc.psum_tensor("ps_sm", [P, 512], F32))
            ps_ab = stk.enter_context(nc.psum_tensor("ps_ab", [P, 2, 512], F32))
            ps_at = stk.enter_context(nc.psum_tensor("ps_at", [P, 2, 512], F32))
            ps_t2 = stk.enter_context(nc.psum_tensor("ps_t2", [P, 512], F32))
            ctx = dict(strip0=strip0, strip1=strip1, negrow=negrow, stt=stt, r2h=r2h, ps_st=ps_st, ps_sm=ps_sm)

            self.new_phase()
            add = self.add
            o_wkv, o_wsrow = self._wrows(add, wkv_bf, self.wkvd, FKV, wskv_r, ps_ws)
            hist = []
            for c in range(NKC):
                H = {}
                hist.append(H)
                H["cast"] = []
                for qq in range(4):
                    gi = 4 * c + qq
                    wws = []
                    if gi >= 2:
                        cc, q2 = divmod(gi - 2, 4)
                        wws.append(hist[cc]["cast"][q2])
                    if qq == 0 and c >= 2:
                        wws += [hist[c - 2]["projlast"], hist[c - 2]["sq"]]
                    ld = add("sync", lambda e, c=c, qq=qq, gi=gi: e.dma_start(
                        xstage[:, gi % 2, :, :],
                        self.xk.ap().rearrange("(m p) t -> p m t", p=P)[
                            :, :, c * 512 + qq * 128:c * 512 + (qq + 1) * 128]),
                        waits=wws, dma="xs%d" % (gi % 2))
                    H["cast"].append(add("scalar", lambda e, c=c, qq=qq, gi=gi: e.copy(
                        xbf[:, c % 2, :, qq * 128:(qq + 1) * 128],
                        xstage[:, gi % 2, :, :]), waits=[ld]))
                sq_w = [H["cast"][3]] + ([hist[c - 1]["stats_s2"]] if c >= 1 else [])
                H["sq"] = add("scalar", lambda e, c=c: e.activation(
                    xsq[:, :, :], xbf[:, c % 2, :, :], AF.Square), waits=sq_w)
                war_st = [hist[c - 1]["strip_done"]] if c >= 1 else []
                war_sm = [hist[c - 1]["r2last"]] if c >= 1 else []
                war_nr = [hist[c - 1]["projlast"]] if c >= 1 else []
                war_strip = [hist[c - 1]["stT_done"]] if c >= 1 else []
                o_nr, o_r2, (o_negmm, o_s2mm) = self._stats_chunk(
                    add, ctx, xbf[:, c % 2, :, :], xsq[:, :, :],
                    [H["cast"][3]], [H["sq"]], war_st, war_sm, war_nr, war_strip)
                H["negrow"] = o_nr
                H["r2last"] = o_r2
                H["stats_s2"] = o_s2mm
                H["strip_done"] = o_nr
                war_form = [hist[c - 2]["formlast"]] if c >= 2 else []
                projs, evs = self._proj_build(add, hist, c, xbf, wkv_bf, wskv_r,
                                              negrow, ps_ab, abkv_f, FKVM, 5,
                                              H["cast"][3], o_nr, war_form)
                if c == 0:
                    projs[0].waits.append(o_wsrow)
                H["proj"] = projs
                H["projlast"] = projs[4]
                H["abev"] = evs
                H["abev_all"] = evs
                cs_w = [hist[c - 2]["rotlast"]] if c >= 2 else []
                o_cos = add("sync", lambda e, c=c: e.dma_start(
                    costile[:, c % 2, :, :],
                    self.cosb.ap().rearrange("(c tt p) w -> c p tt w", p=P, tt=4)[c]),
                    waits=cs_w, dma="cos")
                o_sin = add("sync", lambda e, c=c: e.dma_start(
                    sintile[:, c % 2, :, :],
                    self.sinb.ap().rearrange("(c tt p) w -> c p tt w", p=P, tt=4)[c]),
                    dma="sin")
                H["abTev"] = []
                formops = []
                vformops = []
                stT_done = None
                for tt in range(4):
                    gt = c * 4 + tt
                    tws = list(H["abev"])
                    if gt >= 2:
                        pc, pt = divmod(gt - 2, 4)
                        tws.append(hist[pc]["abTev"][pt])
                    tset = []
                    for mc in range(5):
                        mw = FKVM[mc]
                        if mc < 4:
                            dst = ps_at[:, gt % 2, mc * 128:(mc + 1) * 128]
                            extra = []
                        else:
                            dst = ps_t2[:, tt * 64:(tt + 1) * 64]
                            extra = ([H["abTev"][tt - 1]] if tt >= 1 else
                                     ([hist[c - 1]["abTev"][3]] if c >= 1 else []))
                        tset.append(add("tensor", lambda t, c=c, mc=mc, mw=mw, tt=tt, dst=dst: t.transpose(
                            dst, abkv_f[0:mw, c % 2, mc, tt * 128:(tt + 1) * 128],
                            self.ident_s[0:mw, 0:mw]),
                            waits=(tws if mc == 0 else [H["abev"][mc]]) + extra))
                    stT_done = tset[4]
                    ev_w = tset + ([hist[c - 2]["formlast"], hist[c - 2]["vformlast"]] if c >= 2 else [])
                    e1 = add("vector", lambda v, c=c, tt=tt: v.tensor_copy(
                        abkv_tm[:, c % 2, tt, 0:512], ps_at[:, (c * 4 + tt) % 2, :]), waits=ev_w)
                    e2 = add("vector", lambda v, c=c, tt=tt: v.tensor_copy(
                        abkv_tm[:, c % 2, tt, 512:576], ps_t2[:, tt * 64:(tt + 1) * 64]))
                    H["abTev"].append(e2)
                    bview = abkv_tm[:, c % 2, tt, :].rearrange("p (r ch) -> p r ch", ch=CH)
                    x1v, x2v = bview[:, :, 16:80], bview[:, :, 80:144]
                    cosv = costile[:, c % 2, tt, :].unsqueeze(1).broadcast_to([P, 4, 64])
                    sinv = sintile[:, c % 2, tt, :].unsqueeze(1).broadcast_to([P, 4, 64])
                    add("vector", lambda v, x1v=x1v, cosv=cosv: v.tensor_mul(rott[:, 0], x1v, cosv),
                        waits=[e1, e2, o_cos, o_sin])
                    add("vector", lambda v, x2v=x2v, sinv=sinv: v.tensor_mul(rott[:, 1], x2v, sinv))
                    add("vector", lambda v, x1v=x1v, sinv=sinv: v.tensor_mul(rott[:, 2], x1v, sinv))
                    add("vector", lambda v, x2v=x2v, cosv=cosv: v.tensor_mul(rott[:, 3], x2v, cosv))
                    add("vector", lambda v, x1v=x1v: v.tensor_add(x1v, rott[:, 0], rott[:, 1]))
                    H["rotlast"] = add("vector", lambda v, x2v=x2v: v.tensor_sub(x2v, rott[:, 3], rott[:, 2]))
                    av = bview[:, :, 0:16]
                    o_sc = add("vector", lambda v, av=av, tt=tt: v.tensor_scalar_mul(
                        av, av, r2h[:, tt:tt + 1]), waits=[o_r2])
                    spill_wait = hist[c - 1]["spill"] if c >= 1 else []
                    a0v = bview[:, 0, 0:16].unsqueeze(2).broadcast_to([P, 16, 128])
                    b0v = bview[:, 0, 16:144].unsqueeze(1).broadcast_to([P, 16, 128])
                    a1v = bview[:, 1, 0:16].unsqueeze(2).broadcast_to([P, 16, 128])
                    b1v = bview[:, 1, 16:144].unsqueeze(1).broadcast_to([P, 16, 128])
                    dv = k_tm[:, tt, :].rearrange("p (h dd) -> p h dd", h=16)
                    f1v = ftmp1[:, :].rearrange("p (h dd) -> p h dd", h=16)
                    f2v = ftmp2[:, :].rearrange("p (h dd) -> p h dd", h=16)
                    add("vector", lambda v, a0v=a0v, b0v=b0v, f1v=f1v: v.tensor_mul(f1v, a0v, b0v),
                        waits=[o_sc] + spill_wait)
                    add("vector", lambda v, a1v=a1v, b1v=b1v, f2v=f2v: v.tensor_mul(f2v, a1v, b1v))
                    formops.append(add("vector", lambda v, dv=dv, f1v=f1v, f2v=f2v: v.tensor_add(dv, f1v, f2v)))
                    a2v = bview[:, 2, 0:16].unsqueeze(2).broadcast_to([P, 16, 128])
                    b2v = bview[:, 2, 16:144].unsqueeze(1).broadcast_to([P, 16, 128])
                    a3v = bview[:, 3, 0:16].unsqueeze(2).broadcast_to([P, 16, 128])
                    b3v = bview[:, 3, 16:144].unsqueeze(1).broadcast_to([P, 16, 128])
                    vv = v_tm[:, tt, :].rearrange("p (h dd) -> p h dd", h=16)
                    vt = vtmp[:, :].rearrange("p (h dd) -> p h dd", h=16)
                    add("gpsimd", lambda g, a2v=a2v, b2v=b2v, vt=vt: g.tensor_mul(vt, a2v, b2v),
                        waits=[o_sc] + spill_wait)
                    add("gpsimd", lambda g, a3v=a3v, b3v=b3v, vv=vv: g.tensor_mul(vv, a3v, b3v))
                    vformops.append(add("gpsimd", lambda g, vv=vv, vt=vt: g.tensor_add(vv, vv, vt)))
                H["stT_done"] = stT_done
                H["formlast"] = formops[-1]
                H["vformlast"] = vformops[-1]
                o_ksp = add("sync", lambda e, c=c: e.dma_start(
                    self.ksc.ap().rearrange("(c tt p) f -> c p tt f", p=P, tt=4)[c],
                    k_tm[:, :, :]), waits=[H["formlast"]], dma="ksp")
                o_vsp = add("sync", lambda e, c=c: e.dma_start(
                    self.vsc.ap().rearrange("(c tt p) f -> c p tt f", p=P, tt=4)[c],
                    v_tm[:, :, :]), waits=[H["vformlast"]], dma="vsp")
                H["spill"] = [o_ksp, o_vsp]
            self.emit("p1")

    def p2(self):
        nc = self.nc
        with contextlib.ExitStack() as stk:
            E = lambda *a: stk.enter_context(nc.sbuf_tensor(*a))
            wq_bf = E("wq_bf", [P, 16, FQ], BF16)
            wsq_r = E("wsq_r", [1, FQ], BF16)
            wbuf = E("wbuf", [P, 2, 1366], F32)
            parts3 = E("parts3", [P, 4], F32)
            totr = E("totr", [1, 4], F32)
            srow = E("srow", [1, 8], F32)
            xstage = E("q_xstage", [P, 2, 16, 128], F32)
            xbf = E("q_xbf", [P, 2, 16, 512], BF16)
            xsq = E("q_xsq", [P, 16, 512], BF16)
            strip0 = E("q_strip0", [1, 512], F32)
            strip1 = E("q_strip1", [1, 512], F32)
            negrow = E("q_negrow", [1, 512], BF16)
            stt = E("q_stt", [P, 8], F32)
            r2h = E("q_r2h", [P, 4], F32)
            abq_f = E("abq_f", [P, 2, 7, 512], F32)
            abq_tm = E("abq_tm", [P, 2, 4, FQ], F32)
            ftmp1 = E("q_ftmp1", [P, 2048], F32)
            ftmp2 = E("q_ftmp2", [P, 2048], F32)
            qtmp1 = E("q_qtmp1", [P, 2048], F32)
            qtmp2 = E("q_qtmp2", [P, 2048], F32)
            q_tm = E("q_tm", [P, 2, 2048], BF16)
            ps_st = stk.enter_context(nc.psum_tensor("q_ps_st", [P, 512], F32))
            ps_sm = stk.enter_context(nc.psum_tensor("q_ps_sm", [P, 512], F32))
            ps_ab = stk.enter_context(nc.psum_tensor("q_ps_ab", [P, 2, 512], F32))
            ps_at = stk.enter_context(nc.psum_tensor("q_ps_at", [P, 2, 512], F32))
            ps_at2 = stk.enter_context(nc.psum_tensor("q_ps_at2", [P, 2, 512], F32))
            ctx = dict(strip0=strip0, strip1=strip1, negrow=negrow, stt=stt, r2h=r2h, ps_st=ps_st, ps_sm=ps_sm)

            self.new_phase()
            add = self.add
            o_wq, o_wsrow = self._wrows(add, wq_bf, self.wqd, FQ, wsq_r, ps_sm[0:1, 0:512])
            qspills = []
            o_pz = add("vector", lambda v: v.memset(self.parts[:, :], 0.0))
            widx = []
            for wi, wt in enumerate([self.w1t, self.w3t, self.w2t]):
                ncol = wt.shape[1]
                for ch in range(cdiv(wt.shape[0], P)):
                    for c0 in range(0, ncol, 1366):
                        widx.append((wi, ch, c0, min(1366, ncol - c0), wt))
            sstate = {"i": 0, "acc": {}}

            def pump_spass(n):
                for _ in range(n):
                    i = sstate["i"]
                    if i >= len(widx):
                        return
                    wi, ch, c0, cw, wt = widx[i]
                    r0, rw = ch * P, min(P, wt.shape[0] - ch * P)
                    sl = i % 2
                    ld = add("sync", lambda e, wt=wt, r0=r0, rw=rw, c0=c0, cw=cw, sl=sl:
                             e.dma_start(
                        wbuf[0:rw, sl, 0:cw], wt[r0:r0 + rw, c0:c0 + cw]),
                        waits=[sstate["acc"].get(i - 2)], dma="wb%d" % sl)
                    sstate["acc"][i] = add("scalar", lambda e, rw=rw, cw=cw, sl=sl, i=i:
                                           e.activation(
                        wbuf[0:rw, sl, 0:cw], wbuf[0:rw, sl, 0:cw], AF.Abs,
                        accum_out=self.parts[0:rw, i:i + 1]), waits=[ld, o_pz])
                    sstate["i"] = i + 1
            hist = []
            for c in range(NQC):
                H = {}
                hist.append(H)
                H["cast"] = []
                for qq in range(4):
                    gi = 4 * c + qq
                    wws = []
                    if gi >= 2:
                        cc, q2 = divmod(gi - 2, 4)
                        wws.append(hist[cc]["cast"][q2])
                    if qq == 0 and c >= 2:
                        wws += [hist[c - 2]["projlast"], hist[c - 2]["sq"]]
                    ld = add("sync", lambda e, c=c, qq=qq, gi=gi: e.dma_start(
                        xstage[:, gi % 2, :, :],
                        self.xqT.ap().rearrange("(m p) t -> p m t", p=P)[
                            :, :, c * 512 + qq * 128:c * 512 + (qq + 1) * 128]),
                        waits=wws, dma="qxs%d" % (gi % 2))
                    H["cast"].append(add("scalar", lambda e, c=c, qq=qq, gi=gi: e.copy(
                        xbf[:, c % 2, :, qq * 128:(qq + 1) * 128],
                        xstage[:, gi % 2, :, :]), waits=[ld]))
                sq_w = [H["cast"][3]] + ([hist[c - 1]["stats_s2"]] if c >= 1 else [])
                H["sq"] = add("scalar", lambda e, c=c: e.activation(
                    xsq[:, :, :], xbf[:, c % 2, :, :], AF.Square), waits=sq_w)
                war_st = [hist[c - 1]["strip_done"]] if c >= 1 else []
                war_sm = [hist[c - 1]["stT_rd"]] if c >= 1 else [o_wsrow]
                war_nr = [hist[c - 1]["projlast"]] if c >= 1 else []
                war_strip = [hist[c - 1]["stT_done"]] if c >= 1 else []
                o_nr, o_r2, (o_negmm, o_s2mm) = self._stats_chunk(
                    add, ctx, xbf[:, c % 2, :, :], xsq[:, :, :],
                    [H["cast"][3]], [H["sq"]], war_st, war_sm, war_nr, war_strip)
                o_r2b = add("vector", lambda v: v.tensor_scalar_mul(
                    r2h[:, 0:4], r2h[:, 0:4], float(1.0 / 3.0)))
                H["negrow"] = o_nr
                H["stats_s2"] = o_s2mm
                H["strip_done"] = o_nr
                H["stT_rd"] = o_r2
                war_form = [hist[c - 2]["formlast"]] if c >= 2 else []
                projs, evs = self._proj_build(add, hist, c, xbf, wq_bf, wsq_r,
                                              negrow, ps_ab, abq_f, FQM, 7,
                                              H["cast"][3], o_nr, war_form)
                if c == 0:
                    projs[0].waits.append(o_wsrow)
                H["proj"] = projs
                H["projlast"] = projs[6]
                H["abev"] = evs
                H["abev_all"] = evs
                H["abTev"] = []
                formops = []
                stT_done = None
                for tt in range(4):
                    gt = c * 4 + tt
                    tws = list(H["abev"])
                    if gt >= 2:
                        pc, pt = divmod(gt - 2, 4)
                        tws.append(hist[pc]["abTev"][pt])
                    tset = []
                    for mc in range(7):
                        mw = FQM[mc]
                        if mc < 4:
                            dst = ps_at[:, gt % 2, mc * 128:(mc + 1) * 128]
                        else:
                            dst = ps_at2[:, gt % 2, (mc - 4) * 128:(mc - 4) * 128 + mw]
                        tset.append(add("tensor", lambda t, c=c, mc=mc, mw=mw, tt=tt, dst=dst: t.transpose(
                            dst, abq_f[0:mw, c % 2, mc, tt * 128:(tt + 1) * 128],
                            self.ident_s[0:mw, 0:mw]),
                            waits=(tws if mc == 0 else [H["abev"][mc]])))
                    stT_done = tset[-1]
                    ev_w = tset + ([hist[c - 2]["formlast"]] if c >= 2 else [])
                    e1 = add("vector", lambda v, c=c, tt=tt: v.tensor_copy(
                        abq_tm[:, c % 2, tt, 0:512], ps_at[:, (c * 4 + tt) % 2, :]), waits=ev_w)
                    e2 = add("vector", lambda v, c=c, tt=tt: v.tensor_copy(
                        abq_tm[:, c % 2, tt, 512:864], ps_at2[:, (c * 4 + tt) % 2, 0:352]))
                    H["abTev"].append(e2)
                    bview = abq_tm[:, c % 2, tt, :].rearrange("p (r ch) -> p r ch", ch=CH)
                    av = bview[:, :, 0:16]
                    o_sc = add("vector", lambda v, av=av, tt=tt: v.tensor_scalar_mul(
                        av, av, r2h[:, tt:tt + 1]), waits=[o_r2b, e1, e2])
                    spill_wait = []
                    dv = q_tm[:, tt % 2, :].rearrange("p (h dd) -> p h dd", h=16)
                    f1v = ftmp1[:, :].rearrange("p (h dd) -> p h dd", h=16)
                    f2v = ftmp2[:, :].rearrange("p (h dd) -> p h dd", h=16)
                    t1v = qtmp1[:, :].rearrange("p (h dd) -> p h dd", h=16)
                    t2v = qtmp2[:, :].rearrange("p (h dd) -> p h dd", h=16)
                    def bc(r, ab):
                        if ab == 0:
                            return bview[:, r, 0:16].unsqueeze(2).broadcast_to([P, 16, 128])
                        return bview[:, r, 16:144].unsqueeze(1).broadcast_to([P, 16, 128])
                    add("vector", lambda v, a=bc(0, 0), b=bc(0, 1), f1v=f1v: v.tensor_mul(f1v, a, b),
                        waits=[o_sc] + spill_wait)
                    add("vector", lambda v, a=bc(1, 0), b=bc(1, 1), f2v=f2v: v.tensor_mul(f2v, a, b))
                    add("vector", lambda v, f1v=f1v, f2v=f2v: v.tensor_add(f1v, f1v, f2v))
                    add("vector", lambda v, a=bc(2, 0), b=bc(2, 1), f2v=f2v: v.tensor_mul(f2v, a, b))
                    o_dvp = add("vector", lambda v, f1v=f1v, f2v=f2v: v.tensor_add(f1v, f1v, f2v))
                    add("gpsimd", lambda g, a=bc(3, 0), b=bc(3, 1), t1v=t1v: g.tensor_mul(t1v, a, b),
                        waits=[o_sc] + spill_wait + (formops[-1:] if formops else []))
                    add("gpsimd", lambda g, a=bc(4, 0), b=bc(4, 1), t2v=t2v: g.tensor_mul(t2v, a, b))
                    add("gpsimd", lambda g, t1v=t1v, t2v=t2v: g.tensor_add(t1v, t1v, t2v))
                    add("gpsimd", lambda g, a=bc(5, 0), b=bc(5, 1), t2v=t2v: g.tensor_mul(t2v, a, b))
                    o_plp = add("gpsimd", lambda g, t1v=t1v, t2v=t2v: g.tensor_add(t1v, t1v, t2v))
                    formops.append(add("vector", lambda v, dv=dv, f1v=f1v, t1v=t1v: v.tensor_add(dv, f1v, t1v),
                                       waits=[o_plp] + ([qspills[gt - 2]] if gt >= 2 else [])))
                    qspills.append(add("sync", lambda e, c=c, tt=tt: e.dma_start(
                        self.qsc.ap().rearrange("(c tt p) f -> c tt p f", p=P, tt=4)[c, tt],
                        q_tm[:, tt % 2, :]), waits=[formops[-1]], dma="qsp%d" % (tt % 2)))
                    pump_spass(14)
                H["stT_done"] = stT_done
                H["formlast"] = formops[-1]
                H["spill"] = [qspills[-1]]
            pump_spass(len(widx))
            rng_w = {}
            pos = 0
            for wi, wt in enumerate([self.w1t, self.w3t, self.w2t]):
                n = cdiv(wt.shape[0], P) * cdiv(wt.shape[1], 1366)
                rng_w[wi] = (pos, pos + n)
                pos += n
            p3f = []
            for wi in range(3):
                lo, hi = rng_w[wi]
                p3f.append(add("vector", lambda v, wi=wi, lo=lo, hi=hi: v.tensor_reduce(
                    parts3[:, wi:wi + 1], self.parts[:, lo:hi], AX.X, ALU.add),
                    waits=[sstate["acc"][len(widx) - 1]]))
            o_z3 = add("vector", lambda v: v.memset(parts3[:, 3:4], 0.0))
            sm_guard = [hist[NQC - 1]["stT_rd"], hist[NQC - 1]["abTev"][3]]
            o_totmm = add("tensor", lambda t: t.matmul(
                ps_sm[0:4, 8:9], parts3[:, 0:4], self.ones_cf[:, :], start=True, stop=True),
                waits=p3f + [o_z3] + sm_guard)
            o_totcp = add("vector", lambda v: v.tensor_copy(parts3[0:4, 3:4], ps_sm[0:4, 8:9]),
                          waits=[o_totmm])
            o_totT = add("tensor", lambda t: t.transpose(
                ps_sm[0:1, 12:16], parts3[0:4, 3:4], self.ident_s[0:4, 0:4]),
                waits=[o_totcp])
            o_totr = add("vector", lambda v: v.tensor_copy(totr[0:1, 0:4], ps_sm[0:1, 12:16]),
                         waits=[o_totT])
            cpos = float(0.5 / NW)
            cneg = float(-0.5 / NW)
            sr = [add("vector", lambda v: v.tensor_scalar_mul(srow[0:1, 0:1], totr[0:1, 0:1], cpos), waits=[o_totr])]
            sr.append(add("vector", lambda v: v.tensor_scalar_mul(srow[0:1, 1:2], totr[0:1, 0:1], cneg)))
            sr.append(add("vector", lambda v: v.tensor_scalar_mul(srow[0:1, 2:3], totr[0:1, 1:2], cpos)))
            sr.append(add("vector", lambda v: v.tensor_scalar_mul(srow[0:1, 3:4], totr[0:1, 1:2], cneg)))
            sr.append(add("vector", lambda v: v.tensor_scalar_mul(srow[0:1, 4:5], totr[0:1, 2:3], cpos)))
            sr.append(add("vector", lambda v: v.tensor_scalar_mul(srow[0:1, 5:6], totr[0:1, 2:3], cneg)))
            sr.append(add("vector", lambda v: v.tensor_scalar_mul(srow[0:1, 6:7], totr[0:1, 0:1], float(1.0 / NW))))
            sr.append(add("vector", lambda v: v.tensor_mul(srow[0:1, 7:8], totr[0:1, 1:2], totr[0:1, 2:3])))
            sr.append(add("vector", lambda v: v.tensor_scalar_mul(srow[0:1, 7:8], srow[0:1, 7:8], float(1.0 / NW / NW))))
            o_bc0 = add("tensor", lambda t: t.matmul(
                ps_sm[:, 16:24], self.ones_rf[:, :], srow[0:1, 0:8], start=True, stop=True),
                waits=sr)
            add("vector", lambda v: v.tensor_copy(self.scal[:, 0:8], ps_sm[:, 16:24]), waits=[o_bc0])
            self.emit("p2")

    def dbg_dump(self):
        self.new_phase()
        add = self.add
        if self.dbg == "kv":
            add("sync", lambda e: e.dma_start(self.okv_k[:, :], self.ksc[:, :]), dma="dbg")
            add("sync", lambda e: e.dma_start(self.okv_v[:, :], self.vsc[:, :]), dma="dbg")
        elif self.dbg == "q":
            add("sync", lambda e: e.dma_start(self.oq[:, :], self.qsc[:, :]), dma="dbg")
        nc = self.nc
        with nc.sbuf_tensor("zb", [P, D], F32) as zb:
            zz = add("vector", lambda v: v.memset(zb[:, :], 0.0))
            for i in range(QL // P):
                add("sync", lambda e, i=i: e.dma_start(self.yd[i * P:(i + 1) * P, :], zb[:, :]),
                    waits=[zz], dma="dbg")
            self.emit("dbg")

    def p3(self):
        nc = self.nc
        self._stk_attn = contextlib.ExitStack()
        self.attn_y = self._stk_attn.enter_context(nc.sbuf_tensor("attn_y", [P, 16, QL], BF16))
        with contextlib.ExitStack() as stk:
            E = lambda *a: stk.enter_context(nc.sbuf_tensor(*a))
            Kh = E("Kh", [P, 2, T], BF16)
            Vh = E("Vh", [P, 2, 64, 128], BF16)
            Qh = E("Qh", [P, 2, QL], BF16)
            Pb = E("Pb", [P, 16, 512], BF16)
            mask_s = E("mask_s", [P, 16, 512], BF16)
            rl_f = E("rl_f", [1, 512], F32)
            rlb = E("rlb", [P, 2, 512], F32)
            wstg = E("wstg", [P, 4, 8, 128], F32)
            wqst = E("wqst", [P, 2, 8, 128], BF16)
            qscr = E("qscr", [P, 8, 128], BF16)
            ps_s = stk.enter_context(nc.psum_tensor("ps_s", [P, 2, 1024], F32))
            ps_o = stk.enter_context(nc.psum_tensor("ps_o", [P, 2, 512], F32))
            ps_l = stk.enter_context(nc.psum_tensor("ps_l", [P, 512], F32))
            ps_rb = stk.enter_context(nc.psum_tensor("ps_rb", [P, 512], F32))

            self.new_phase()
            add = self.add
            o_mask = add("sync", lambda e: e.dma_start(
                mask_s[:, :, :], self.dmaskd.ap().rearrange("kl i j -> i kl j")), dma="mask")

            o_wz = add("vector", lambda v: v.memset(wstg[:, :, :, :], 0.0))
            qtasks = []
            for wi, (wt, wsc) in enumerate(((self.w1t, self.wq1sc), (self.w3t, self.wq3sc))):
                for fc in range(NFC):
                    fw = 128 if fc < NFC - 1 else FC_LAST
                    for half in (0, 1):
                        qtasks.append(("a", wi, wt, wsc, fc, fw, half))
            for dc in range(16):
                for f0 in range(0, NFC, 8):
                    qtasks.append(("b", dc, f0, min(8, NFC - f0)))
            qstate = {"i": 0, "q2": {}, "sp": {}}

            def pump_quant(n):
                for _ in range(n):
                    i = qstate["i"]
                    if i >= len(qtasks):
                        return
                    task = qtasks[i]
                    par = i % 4
                    spar = i % 2
                    ldw = [qstate["q2"].get(i - 4, o_wz)]
                    if task[0] == "a":
                        _, wi, wt, wsc, fc, fw, half = task
                        nc_ = 8 * 128
                        lds = [add("sync", lambda e, wt=wt, fc=fc, fw=fw, half=half, par=par:
                                   e.dma_start(
                            wstg[:, par, :, 0:fw],
                            wt.ap().rearrange("(m p) h -> p m h", p=P)[
                                :, half * 8:half * 8 + 8, fc * 128:fc * 128 + fw]),
                            waits=ldw, dma="qw%d" % par)]
                        pc, ngc = 2 * wi, 2 * wi + 1
                        dst = lambda e, wsc=wsc, fc=fc, half=half, spar=spar: e.dma_start(
                            wsc[fc][:, half * 1024:(half + 1) * 1024], wqst[:, spar, :, :])
                    else:
                        _, dc, f0, fn = task
                        nfull = fn if f0 + fn <= NFC - 1 else fn - 1
                        nc_ = fn * 128
                        lds = [add("sync", lambda e, dc=dc, f0=f0, nfull=nfull, par=par:
                                   e.dma_start(
                            wstg[:, par, 0:nfull, :],
                            self.w2t.ap()[0:(NFC - 1) * P, :].rearrange(
                                "(fc p) d -> p fc d", p=P)[:, f0:f0 + nfull,
                                dc * 128:(dc + 1) * 128]),
                            waits=ldw, dma="qw%d" % par)]
                        if nfull < fn:
                            lds.append(add("sync", lambda e, dc=dc, nfull=nfull, par=par:
                                           e.dma_start(
                                wstg[0:FC_LAST, par, nfull, :],
                                self.w2t[(NFC - 1) * P:HID, dc * 128:(dc + 1) * 128]),
                                waits=ldw, dma="qwt"))
                        pc, ngc = 4, 5
                        dst = lambda e, dc=dc, f0=f0, fn=fn, spar=spar: e.dma_start(
                            self.wq2sc[dc][:, f0 * 128:(f0 + fn) * 128],
                            wqst[:, spar, 0:fn, :])
                    add("vector", lambda v, par=par, nc_=nc_, ngc=ngc: v.tensor_scalar(
                        qscr.ap().rearrange("p a b -> p (a b)")[:, 0:nc_],
                        wstg.ap().rearrange("p s a b -> p s (a b)")[:, par, 0:nc_],
                        self.scal[:, ngc:ngc + 1], -1.0, ALU.is_gt, ALU.add),
                        waits=lds + [qstate["sp"].get(i - 2)])
                    o_q2 = add("vector", lambda v, par=par, spar=spar, nc_=nc_, pc=pc: v.scalar_tensor_tensor(
                        wqst.ap().rearrange("p s a b -> p s (a b)")[:, spar, 0:nc_],
                        wstg.ap().rearrange("p s a b -> p s (a b)")[:, par, 0:nc_],
                        self.scal[:, pc:pc + 1],
                        qscr.ap().rearrange("p a b -> p (a b)")[:, 0:nc_],
                        ALU.is_gt, ALU.add))
                    qstate["q2"][i] = o_q2
                    qstate["sp"][i] = add("sync", dst, waits=[o_q2], dma="qs%d" % spar)
                    qstate["i"] = i + 1

            exp_hist = []
            pv_cover = []
            l_cover = []
            prev_oev = {0: None, 1: None}
            prev_rl = {0: None, 1: None}
            prev_rbe = None
            prev_orb = [None]
            hist = []
            gi = 0
            for h in range(NH):
                H = {}
                hist.append(H)
                lw = [hist[h - 2]["hdone"]] if h >= 2 else []
                o_K = add("sync", lambda e, h=h: e.dma_start(
                    Kh[:, h % 2, :], self.ksc[:, h * 128:(h + 1) * 128], transpose=True),
                    waits=lw, dma="K%d" % (h % 2))
                o_V = add("sync", lambda e, h=h: e.dma_start(
                    Vh[:, h % 2, :, :],
                    self.vsc.ap().rearrange("(kt p) f -> p kt f", p=P)[:, :, h * 128:(h + 1) * 128]),
                    waits=lw, dma="V%d" % (h % 2))
                o_Q = add("sync", lambda e, h=h: e.dma_start(
                    Qh[:, h % 2, :], self.qsc[:, h * 128:(h + 1) * 128], transpose=True),
                    waits=lw, dma="Q%d" % (h % 2))
                qcd = []
                for qc in range(NQC):
                    nkt = 16 * (qc + 1)
                    o_pv = None
                    o_l = None
                    def flush(pend, qc=qc, nkt=nkt):
                        (kt, ib, pos, o_e) = pend
                        grp_start = (pos == 0)
                        grp_stop = (pos == nkt - 1)
                        pw = [o_e, o_V]
                        if grp_start:
                            pw.append(prev_oev[qc % 2])
                        o_pv = add("tensor", lambda t, h=h, qc=qc, kt=kt, ib=ib,
                                   grp_start=grp_start, grp_stop=grp_stop: t.matmul(
                            ps_o[:, qc % 2, :], Vh[:, h % 2, kt, :], Pb[:, ib % 16, :],
                            start=grp_start, stop=grp_stop), waits=pw)
                        lwz = [o_e]
                        if grp_start:
                            lwz.append(prev_rl[0])
                        o_l = add("tensor", lambda t, qc=qc, ib=ib,
                                  grp_start=grp_start, grp_stop=grp_stop: t.matmul(
                            ps_l[0:1, :], self.ones_c[:, :], Pb[:, ib % 16, :],
                            start=grp_start, stop=grp_stop), waits=lwz)
                        pv_cover.append(o_pv)
                        l_cover.append(o_l)
                        return o_pv, o_l
                    pend = []
                    if qc == 0:
                        kt_order = list(range(nkt))
                    else:
                        kt_order = []
                        for i in range(16):
                            kt_order.extend(range(i * qc, (i + 1) * qc))
                            kt_order.append(16 * qc + i)
                    for pos, kt in enumerate(kt_order):
                        ib = gi
                        sws = [o_K, o_Q]
                        if ib >= 4:
                            sws.append(exp_hist[ib - 4])
                        o_s = add("tensor", lambda t, h=h, qc=qc, kt=kt, ib=ib: t.matmul(
                            ps_s[:, (ib // 2) % 2, (ib % 2) * 512:(ib % 2) * 512 + 512],
                            Kh[:, h % 2, kt * 128:(kt + 1) * 128],
                            Qh[:, h % 2, qc * 512:(qc + 1) * 512], start=True, stop=True),
                            waits=sws)
                        ews = [o_s]
                        if ib >= 16:
                            ews += [pv_cover[ib - 16], l_cover[ib - 16]]
                        o_ex = add("scalar", lambda e, ib=ib: e.activation(
                            Pb[:, ib % 16, :],
                            ps_s[:, (ib // 2) % 2, (ib % 2) * 512:(ib % 2) * 512 + 512],
                            AF.Exp, bias=f32(EXPB), scale=f32(SCALE)), waits=ews)
                        o_e = o_ex
                        if kt >= 16 * qc:
                            kl = kt - 16 * qc
                            meng = "vector" if qc == 0 else "gpsimd"
                            o_e = add(meng, lambda v, ib=ib, kl=kl: v.tensor_mul(
                                Pb[:, ib % 16, :], Pb[:, ib % 16, :], mask_s[:, kl, :]),
                                waits=[o_ex, o_mask])
                        exp_hist.append(o_ex)
                        pend.append((kt, ib, pos, o_e))
                        if pos == nkt // 2:
                            pump_quant(3)
                        if len(pend) > 2:
                            o_pv, o_l = flush(pend.pop(0))
                        gi += 1
                    for item in pend:
                        o_pv, o_l = flush(item)
                    o_rl = add("vector", lambda v, qc=qc: v.reciprocal(
                        rl_f[0:1, :], ps_l[0:1, :]),
                        waits=[o_l, prev_orb[0]])
                    prev_rl[0] = o_rl
                    o_rb = add("tensor", lambda t: t.matmul(
                        ps_rb[:, :], self.ones_rf[:, :], rl_f[0:1, :], start=True, stop=True),
                        waits=[o_rl, prev_rbe])
                    prev_orb[0] = o_rb
                    o_rbe = add("vector", lambda v, qc=qc: v.tensor_copy(
                        rlb[:, qc % 2, :], ps_rb[:, :]), waits=[o_rb])
                    prev_rbe = o_rbe
                    o_ev = add("vector", lambda v, h=h, qc=qc: v.tensor_mul(
                        self.attn_y[:, h, qc * 512:(qc + 1) * 512], ps_o[:, qc % 2, :],
                        rlb[:, qc % 2, :]), waits=[o_pv, o_rbe])
                    prev_oev[qc % 2] = o_ev
                    qcd.append(o_ev)
                    pump_quant(2)
                H["hdone"] = qcd[-1]
            assert len(pv_cover) == gi and len(l_cover) == gi
            pump_quant(len(qtasks))
            if self.dbg == "attn":
                add("sync", lambda e: e.dma_start(
                    self.oattn.ap().rearrange("(m p) j -> p m j", p=P), self.attn_y[:, :, :]),
                    waits=[hist[i]["hdone"] for i in range(NH)], dma="dbg")
            self.emit("p3")

    def _ln_tok(self, add, row_in, out_fn, sqt, mrc, deps):
        o_r1 = add("vector", lambda v: v.tensor_reduce(mrc[:, 0:1], row_in, AX.X, ALU.add),
                   waits=deps)
        o_sq = add("scalar", lambda e: e.activation(sqt[:, :], row_in, AF.Square), waits=deps)
        o_r2 = add("vector", lambda v: v.tensor_reduce(mrc[:, 1:2], sqt[:, :], AX.X, ALU.add),
                   waits=[o_sq])
        add("vector", lambda v: v.tensor_scalar_mul(mrc[:, 2:3], mrc[:, 0:1], float(1.0 / D)))
        add("vector", lambda v: v.tensor_mul(mrc[:, 3:4], mrc[:, 2:3], mrc[:, 2:3]))
        o_var = add("vector", lambda v: v.scalar_tensor_tensor(
            mrc[:, 4:5], mrc[:, 1:2], float(1.0 / D), mrc[:, 3:4], ALU.mult, ALU.subtract))
        o_ln = add("scalar", lambda e: e.activation(mrc[:, 5:6], mrc[:, 4:5], AF.Ln,
                                                    bias=f32(EPS)), waits=[o_var])
        o_rs = add("scalar", lambda e: e.activation(mrc[:, 6:7], mrc[:, 5:6], AF.Exp,
                                                    scale=f32(-0.5)))
        return out_fn(mrc[:, 2:3], mrc[:, 6:7], [o_rs])

    def p4(self):
        nc = self.nc
        with contextlib.ExitStack() as stk:
            E = lambda *a: stk.enter_context(nc.sbuf_tensor(*a))
            wob = E("wob", [P, 2, 16, 128], BF16)
            a2e = E("a2e", [P, 2, 512], F32)
            a2q = E("a2q", [P, 4, 2048], F32)
            xqt_s = E("xqt_s", [P, 2, 2048], F32)
            x2t = E("x2t", [P, 2, 2048], F32)
            ln3t = E("ln3t", [P, 2048], F32)
            sqt = E("sqt", [P, 2048], F32)
            mrc = E("mrc", [P, 8], F32)
            ps_a2 = stk.enter_context(nc.psum_tensor("ps_a2", [P, 2, 512], F32))
            ps_tT = stk.enter_context(nc.psum_tensor("ps_tT", [P, 2, 512], F32))
            ps_l3 = stk.enter_context(nc.psum_tensor("ps_l3", [P, 2, 512], F32))

            self.new_phase()
            add = self.add
            hist = []
            prev_wcast = [None]
            xq_hist = []
            xsp_hist = []
            hist_wo = []
            hist_a2ev = []
            hist_tTe = []
            hist_l3e = []
            hist_l3tp = []
            for qc in range(NQC):
                H = {"a2ev": []}
                hist.append(H)

                for dc in range(16):
                    gdc = qc * 16 + dc
                    o_wo = add("gpsimd", lambda e, dc=dc, gdc=gdc: e.dma_start(
                        wob[:, gdc % 2, :, :],
                        self.wod.ap().rearrange("(m p) d -> p m d", p=P)[
                            :, :, dc * 128:(dc + 1) * 128]),
                        waits=[hist_wo[gdc - 2]] if gdc >= 2 else [], dma="wo%d" % (gdc % 2))
                    def mm_a2(t, qc=qc, gdc=gdc):
                        for m in range(16):
                            ins = t.matmul(ps_a2[:, gdc % 2, :], wob[:, gdc % 2, m, :],
                                           self.attn_y[:, m, qc * 512:(qc + 1) * 512],
                                           start=(m == 0), stop=(m == 15))
                        return ins
                    mmw = [o_wo]
                    if gdc >= 2:
                        mmw.append(hist_a2ev[gdc - 2])
                    o_mm = add("tensor", mm_a2, waits=mmw)
                    hist_wo.append(o_mm)
                    o_ev = add("scalar", lambda e, gdc=gdc: e.copy(
                        a2e[:, gdc % 2, :], ps_a2[:, gdc % 2, :]),
                        waits=[o_mm] + ([hist_tTe[(gdc - 2) * 4 + 3]] if gdc >= 2 else []))
                    hist_a2ev.append(o_ev)
                    H["a2ev"].append(o_ev)
                    for tt in range(4):
                        gt = gdc * 4 + tt
                        tpw = [o_ev]
                        if gt >= 2:
                            tpw.append(hist_tTe[gt - 2])
                        o_tp = add("tensor", lambda t, tt=tt, gt=gt, gdc=gdc: t.transpose(
                            ps_tT[:, gt % 2, 0:128], a2e[:, gdc % 2, tt * 128:(tt + 1) * 128],
                            self.ident_s[:, :]), waits=tpw)
                        hist_tTe.append(add("vector", lambda v, tt=tt, dc=dc, gt=gt: v.tensor_copy(
                            a2q[:, tt, dc * 128:(dc + 1) * 128], ps_tT[:, gt % 2, 0:128]),
                            waits=[o_tp]))
                x2dones = []
                for tt in range(4):
                    gxq = qc * 4 + tt
                    o_xq = add("sync", lambda e, qc=qc, tt=tt, gxq=gxq: e.dma_start(
                        xqt_s[:, gxq % 2, :],
                        self.xqd[qc * 512 + tt * 128:qc * 512 + (tt + 1) * 128, :]),
                        waits=[xq_hist[gxq - 2]] if gxq >= 2 else [], dma="xq%d" % (gxq % 2))
                    deps = [hist_tTe[(qc * 16 + dc) * 4 + tt] for dc in range(16)]
                    row = a2q[:, tt, :]
                    def apply_ln2(mcol, rcol, w, row=row, tt=tt, qc=qc, gxq=gxq, o_xq=o_xq):
                        o_n = add("vector", lambda v: v.tensor_scalar(
                            sqt[:, :], row, mcol, rcol, ALU.subtract, ALU.mult), waits=w)
                        return add("vector", lambda v, tt=tt, gxq=gxq: v.tensor_add(
                            x2t[:, gxq % 2, :], xqt_s[:, gxq % 2, :], sqt[:, :]),
                            waits=[o_xq] + ([xsp_hist[gxq - 2]] if gxq >= 2 else []))
                    o_x2 = self._ln_tok(add, row, apply_ln2, sqt, mrc, deps)
                    xq_hist.append(o_x2)
                    H.setdefault("x2add", []).append(o_x2)
                    def apply_ln3(mcol, rcol, w, tt=tt, gxq=gxq):
                        return add("vector", lambda v, gxq=gxq: v.tensor_scalar(
                            ln3t[:, :], x2t[:, gxq % 2, :], mcol, rcol, ALU.subtract, ALU.mult),
                            waits=w + ([hist_l3tp[-1]] if hist_l3tp else []))
                    o_l3 = self._ln_tok(add, x2t[:, gxq % 2, :], apply_ln3, sqt, mrc, [o_x2])
                    for dc2 in range(16):
                        gl = (qc * 4 + tt) * 16 + dc2
                        tpw = [o_l3]
                        if gl >= 2:
                            tpw.append(hist_l3e[gl - 2])
                        o_tp = add("tensor", lambda t, dc2=dc2, gl=gl: t.transpose(
                            ps_l3[:, gl % 2, 0:128], ln3t[:, dc2 * 128:(dc2 + 1) * 128],
                            self.ident_s[:, :]), waits=tpw)
                        hist_l3tp.append(o_tp)
                        hist_l3e.append(add("vector", lambda v, tt=tt, dc2=dc2, qc=qc, gl=gl: v.tensor_copy(
                            self.attn_y[:, dc2, qc * 512 + tt * 128:qc * 512 + (tt + 1) * 128],
                            ps_l3[:, gl % 2, 0:128]), waits=[o_tp, hist_a2ev[qc * 16 + 15]]))
                    o_xsp = add("sync", lambda e, qc=qc, tt=tt, gxq=gxq: e.dma_start(
                        self.x2sc[qc * 512 + tt * 128:qc * 512 + (tt + 1) * 128, :],
                        x2t[:, gxq % 2, :]), waits=[o_l3], dma="x2s%d" % (gxq % 2))
                    xsp_hist.append(o_xsp)
                    x2dones.append(o_xsp)
                H["x2sp"] = x2dones
                H["x2done"] = x2dones[-1]
            add("sync", lambda e: e.dma_start(
                self.ln3sc.ap().rearrange("(m p) j -> p m j", p=P), self.attn_y[:, :, :]),
                waits=[hist_l3e[-1]], dma="l3sp")
            if self.dbg == "x2":
                add("sync", lambda e: e.dma_start(self.ox2[:, :], self.x2sc[:, :]),
                    waits=[t for i in range(NQC) for t in hist[i]["x2sp"]], dma="dbg")
            self.emit("p4")

    def p56(self):
        nc = self.nc
        self._stk_attn.close()
        with contextlib.ExitStack() as stk:
            E = lambda *a: stk.enter_context(nc.sbuf_tensor(*a))
            ln3q = E("ln3q", [P, 16, 512], BF16)
            w1q = E("w1q", [P, 2, 2048], BF16)
            w3q = E("w3q", [P, 2, 2048], BF16)
            gsb = E("gsb", [P, 2, NFC, 512], BF16)
            silu_t = E("silu_t", [P, 512], F32)
            w2q = E("w2q", [P, 4, 22, 128], BF16)
            h2f = E("h2f", [P, 512], F32)
            out_tm = E("out_tm", [P, 4, 2048], F32)
            x2r = E("x2r", [P, 2048], F32)
            sqt = E("sqt2", [P, 2048], F32)
            mrc = E("mrc2", [P, 8], F32)
            ps_h1 = stk.enter_context(nc.psum_tensor("ps_h1", [P, 2, 512], F32))
            ps_h3 = stk.enter_context(nc.psum_tensor("ps_h3", [P, 2, 512], F32))
            ps_h2 = stk.enter_context(nc.psum_tensor("ps_h2", [P, 2, 512], F32))
            ps_oT = stk.enter_context(nc.psum_tensor("ps_oT", [P, 2, 512], F32))

            self.new_phase()
            add = self.add
            prev_q = {}
            mm1_hist = {}
            glast = {}
            q_last_mm1 = {}
            o_l3 = {}
            prev_w2 = {}
            mm2_hist = {}
            hist_oev = []
            ydones = []
            ydones_q = {}
            prev_oy = []
            jobs = []

            def emit_mm2(tq, dc):
                gdc = tq * 16 + dc
                wq2toks = []
                pieces = [(0, 22), (22, 21)]
                for pi, (f0, fn) in enumerate(pieces):
                    gpi = gdc * 2 + pi
                    slot = (dc % 2) * 2 + pi
                    mmwar2 = [mm2_hist.get(gdc - 2)] if pi == 0 else []
                    o_wq2 = add("sync", lambda e, dc=dc, tq=tq, f0=f0, fn=fn, slot=slot: e.dma_start(
                        w2q[:, slot, 0:fn, :],
                        self.wq2sc[dc][:, f0 * 128:(f0 + fn) * 128]),
                        waits=[prev_w2.get(gpi - 4)] + mmwar2, dma="w2_%d" % slot)
                    prev_w2[gpi] = o_wq2
                    wq2toks.append(o_wq2)
                def mmh2(t, tq=tq, dc=dc):
                    for pi, (f0, fn) in enumerate(pieces):
                        slot = (dc % 2) * 2 + pi
                        for fi in range(fn):
                            fc = f0 + fi
                            fw = 128 if fc < NFC - 1 else FC_LAST
                            ins = t.matmul(ps_h2[:, gdc % 2, :],
                                           w2q[0:fw, slot, fi, :],
                                           gsb[0:fw, tq % 2, fc, :],
                                           start=(fc == 0), stop=(fc == NFC - 1))
                    return ins
                mmw = list(wq2toks) + [glast[tq * NFC + NFC - 1]]
                if gdc >= 2 and hist_oev:
                    mmw.append(hist_oev[-1])
                o_mm2 = add("tensor", mmh2, waits=mmw)
                mm2_hist[gdc] = o_mm2
                o_h2e = add("vector", lambda v, gdc=gdc: v.tensor_scalar_mul(
                    h2f[:, :], ps_h2[:, gdc % 2, :], self.scal[:, 7:8]), waits=[o_mm2])
                for tt in range(4):
                    gt = gdc * 4 + tt
                    tpw = [o_h2e]
                    if len(hist_oev) >= 2:
                        tpw.append(hist_oev[-2])
                    o_tp = add("tensor", lambda t, tt=tt, gt=gt: t.transpose(
                        ps_oT[:, gt % 2, 0:128], h2f[:, tt * 128:(tt + 1) * 128],
                        self.ident_s[:, :]), waits=tpw)
                    evw = [o_tp]
                    if dc == 0 and (tq - 1, tt) in ydones_q:
                        evw.append(ydones_q[(tq - 1, tt)])
                    hist_oev.append(add("vector", lambda v, tt=tt, dc=dc, gt=gt: v.tensor_copy(
                        out_tm[:, tt, dc * 128:(dc + 1) * 128], ps_oT[:, gt % 2, 0:128]),
                        waits=evw))

            def emit_ln4(tq):
                x2rl = {}
                for tt in range(4):
                    deps = [hist_oev[tq * 64 + dc * 4 + tt] for dc in range(16)] + prev_oy
                    x2rl[tt] = add("sync", lambda e, tq=tq, tt=tt: e.dma_start(
                        x2r[:, :],
                        self.x2sc[tq * 512 + tt * 128:tq * 512 + (tt + 1) * 128, :]),
                        waits=[ydones[-1]] if ydones else [], dma="x2r")
                    def apply_ln4(mcol, rcol, w, tt=tt, tq=tq):
                        o_n = add("vector", lambda v, tt=tt: v.tensor_scalar(
                            sqt[:, :], out_tm[:, tt, :], mcol, rcol,
                            ALU.subtract, ALU.mult), waits=w)
                        return add("vector", lambda v, tt=tt: v.tensor_add(
                            out_tm[:, tt, :], x2r[:, :], sqt[:, :]), waits=[x2rl[tt]])
                    o_y = self._ln_tok(add, out_tm[:, tt, :], apply_ln4, sqt, mrc, deps)
                    prev_oy[:] = [o_y]
                    yop = add("sync", lambda e, tq=tq, tt=tt: e.dma_start(
                        self.yd[tq * 512 + tt * 128:tq * 512 + (tt + 1) * 128, :],
                        out_tm[:, tt, :]), waits=[o_y], dma="yst")
                    ydones.append(yop)
                    ydones_q[(tq, tt)] = yop

            for tq in range(4):
                o_l3 [tq] = add("sync", lambda e, tq=tq: e.dma_start(
                    ln3q[:, :, :],
                    self.ln3sc.ap().rearrange("(m p) j -> p m j", p=P)[
                        :, :, tq * 512:(tq + 1) * 512]),
                    waits=[q_last_mm1.get(tq - 1)], dma="l3q")
                for _ in range(2):
                    if jobs:
                        jobs.pop(0)()
                for fc in range(NFC):
                    gfc_g = tq * NFC + fc
                    fw = 128 if fc < NFC - 1 else FC_LAST
                    qs = []
                    for wi, (wqsb, wsc) in enumerate(((w1q, self.wq1sc), (w3q, self.wq3sc))):
                        gfc = gfc_g * 2 + wi
                        o_ld = add("sync", lambda e, wqsb=wqsb, wsc=wsc, fc=fc: e.dma_start(
                            wqsb[:, fc % 2, :], wsc[fc]),
                            waits=[prev_q.get(gfc - 4), mm1_hist.get((gfc_g - 2, wi))],
                            dma="wl%d_%d" % (wi, gfc_g % 2))
                        prev_q[gfc] = o_ld
                        qs.append(o_ld)
                    mmo = []
                    for wi, (wqsb, psd) in enumerate(((w1q, ps_h1), (w3q, ps_h3))):
                        def mmh(t, wqsb=wqsb, psd=psd, fc=fc, fw=fw, gfc_g=gfc_g, tq=tq):
                            for m in range(16):
                                ins = t.matmul(
                                    psd[0:fw, gfc_g % 2, :],
                                    wqsb[:, fc % 2, :].rearrange("p (m f) -> p m f", m=16)[:, :, 0:fw][:, m, :],
                                    ln3q[:, m, :],
                                    start=(m == 0), stop=(m == 15))
                            return ins
                        ws = [qs[wi], o_l3[tq], glast.get(gfc_g - 2)]
                        o_mmx = add("tensor", mmh, waits=ws)
                        mm1_hist[(gfc_g, wi)] = o_mmx
                        mmo.append(o_mmx)
                    q_last_mm1[tq] = mmo[-1]
                    o_si = add("scalar", lambda e, gfc_g=gfc_g, fw=fw: e.activation(
                        silu_t[0:fw, :], ps_h1[0:fw, gfc_g % 2, :], AF.Silu,
                        scale=self.scal[0:fw, 6:7]),
                        waits=mmo + ([glast[gfc_g - 1]] if gfc_g >= 1 else []))
                    gw = [o_si]
                    if fc == 0 and tq >= 2:
                        gw.append(mm2_hist[(tq - 2) * 16 + 15])
                    glast[gfc_g] = add("vector", lambda v, tq=tq, fc=fc, gfc_g=gfc_g, fw=fw: v.tensor_mul(
                        gsb[0:fw, tq % 2, fc, :], silu_t[0:fw, :], ps_h3[0:fw, gfc_g % 2, :]),
                        waits=gw)
                    if fc % 2 == 1 and jobs:
                        jobs.pop(0)()
                for dc in range(16):
                    jobs.append(lambda tq=tq, dc=dc: emit_mm2(tq, dc))
                jobs.append(lambda tq=tq: emit_ln4(tq))
            while jobs:
                jobs.pop(0)()
            self.emit("p56")


def host_inputs(inp, core):
    b, r = core // 4, core % 4
    x = inp["L0_attn_x"]
    out = {}
    out["xk"] = np.ascontiguousarray(x[b].T)
    out["xq"] = np.ascontiguousarray(x[b, r::4, :])
    out["xqT"] = np.ascontiguousarray(x[b, r::4, :].T)
    out["wq"] = np.ascontiguousarray(inp["L0_attn_W_q"].T)
    out["wkv"] = np.ascontiguousarray(inp["L0_attn_W_kv"].T)
    out["wo"] = np.ascontiguousarray(inp["L0_attn_W_o"].T)
    out["w1t"] = np.ascontiguousarray(inp["L0_ffn_w1"].T)
    out["w3t"] = np.ascontiguousarray(inp["L0_ffn_w3"].T)
    out["w2t"] = np.ascontiguousarray(inp["L0_ffn_w2"].T)
    inv = (np.float32(1.0) / np.float32(10000.0)) ** (
        np.arange(0, 128, 2, dtype=np.float32) / np.float32(128.0))
    t = np.arange(T, dtype=np.float32)
    fr = t[:, None] * inv[None, :]
    out["cosb"] = np.cos(fr).astype(np.float32)
    out["sinb"] = np.sin(fr).astype(np.float32)
    import ml_dtypes
    kl = np.arange(16)[:, None, None]
    i = np.arange(128)[None, :, None]
    j = np.arange(512)[None, None, :]
    out["dmask"] = ((4 * j + r) >= (128 * kl + i)).astype(ml_dtypes.bfloat16)
    out["ident"] = np.eye(P, dtype=np.float32)
    return out


def gather_y(results):
    y = np.empty((2, T, D), np.float32)
    for core in range(8):
        b, r = core // 4, core % 4
        y[b, r::4, :] = results[core]["y"]
    return y


from concourse.bass_utils import run_bass_kernel_spmd


def kernel(**inputs):
    nc = MK().build()
    in_maps = [host_inputs(inputs, c) for c in range(8)]
    res = run_bass_kernel_spmd(nc, in_maps, core_ids=list(range(8)))
    return gather_y(res.results)


# version 13
# speedup vs baseline: 1.0960x; 1.0960x over previous
import math
import contextlib
import numpy as np
import concourse.bass as bass
import concourse.mybir as mybir

F32 = mybir.dt.float32
BF16 = mybir.dt.bfloat16
AF = mybir.ActivationFunctionType
ALU = mybir.AluOpType
AX = mybir.AxisListType

P = 128
D = 2048
T = 8192
NH = 16
CH = 144
QL = 2048
NKC = 16
NQC = 4
FQ, FKV = 864, 576
FQM = [128] * 6 + [96]
FKVM = [128] * 4 + [64]
HID = 5461
NFC = 43
FC_LAST = HID - 128 * (NFC - 1)
SCALE = float(1.0 / math.sqrt(128.0))
EXPB = -12.0
EPS = 1e-5
NW = HID * D
f32 = np.float32


def cdiv(a, b):
    return (a + b - 1) // b


class Op:
    __slots__ = ("eng", "build", "waits", "sem", "val", "k")


class MK:
    SERIAL = ("vector", "scalar", "gpsimd")

    def __init__(self, dbg=None, phases=("p0", "p1", "p2", "p3", "p4", "p56")):
        self.dbg = dbg
        self.phases = phases
        nc = self.nc = bass.Bass("TRN2", target_bir_lowering=False)
        dr = nc.dram_tensor
        self.xk = dr("xk", [D, T], F32, kind="ExternalInput")
        self.xqd = dr("xq", [QL, D], F32, kind="ExternalInput")
        self.xqT = dr("xqT", [D, QL], F32, kind="ExternalInput")
        self.wqd = dr("wq", [D, FQ], F32, kind="ExternalInput")
        self.wkvd = dr("wkv", [D, FKV], F32, kind="ExternalInput")
        self.wod = dr("wo", [D, D], F32, kind="ExternalInput")
        self.w1t = dr("w1t", [D, HID], F32, kind="ExternalInput")
        self.w3t = dr("w3t", [D, HID], F32, kind="ExternalInput")
        self.w2t = dr("w2t", [HID, D], F32, kind="ExternalInput")
        self.cosb = dr("cosb", [T, 64], F32, kind="ExternalInput")
        self.sinb = dr("sinb", [T, 64], F32, kind="ExternalInput")
        self.dmaskd = dr("dmask", [16, 128, 512], BF16, kind="ExternalInput")
        self.identd = dr("ident", [P, P], F32, kind="ExternalInput")
        self.yd = dr("y", [QL, D], F32, kind="ExternalOutput")
        self.ksc = dr("ksc", [T, D], BF16, kind="Internal")
        self.vsc = dr("vsc", [T, D], BF16, kind="Internal")
        self.qsc = dr("qsc", [QL, D], BF16, kind="Internal")
        self.x2sc = dr("x2sc", [QL, D], F32, kind="Internal")
        self.ln3sc = dr("ln3sc", [D, QL], BF16, kind="Internal")
        self.wq1sc = dr("wq1sc", [NFC, P, 2048], BF16, kind="Internal")
        self.wq3sc = dr("wq3sc", [NFC, P, 2048], BF16, kind="Internal")
        self.wq2sc = dr("wq2sc", [16, P, NFC * 128], BF16, kind="Internal")
        if dbg == "kv":
            self.okv_k = dr("okv_k", [T, D], BF16, kind="ExternalOutput")
            self.okv_v = dr("okv_v", [T, D], BF16, kind="ExternalOutput")
        if dbg == "q":
            self.oq = dr("oq", [QL, D], BF16, kind="ExternalOutput")
        if dbg == "attn":
            self.oattn = dr("oattn", [NH * P, QL], BF16, kind="ExternalOutput")
        if dbg == "x2":
            self.ox2 = dr("ox2", [QL, D], F32, kind="ExternalOutput")

        self.sems = {}
        for name in ("sync", "vector", "tensor", "scalar", "gpsimd"):
            h = nc.semaphore("s_" + name).__enter__()
            self.sems[name] = [h, 0]
        self.marks = {e: {} for e in ("sync", "vector", "tensor", "scalar", "gpsimd")}

        A = nc.alloc_sbuf_tensor
        self.ident_s = A("ident_s", [P, P], F32)
        self.ones_c = A("ones_c", [P, 1], BF16)
        self.nones_c = A("nones_c", [P, 1], BF16)
        self.ones_rf = A("ones_rf", [1, P], F32)
        self.ones_cf = A("ones_cf", [P, 1], F32)
        self.scal = A("scal", [P, 8], F32)
        self.parts = A("parts", [P, 224], F32)

    def new_phase(self):
        self.ops = []
        self.last_on = {}

    def add(self, eng, build, waits=(), k=None, dma=None):
        op = Op()
        op.eng = eng
        op.build = build
        ws = [w for w in waits if w is not None]
        if eng in self.SERIAL and self.last_on.get(eng) is not None:
            ws.append(self.last_on[eng])
        op.waits = ws
        if dma:
            sem_name = "dma_" + dma
            if sem_name not in self.sems:
                h = self.nc.semaphore("s_" + sem_name).__enter__()
                self.sems[sem_name] = [h, 0]
            op.k = 16 if k is None else k
        else:
            sem_name = eng
            op.k = 1 if k is None else k
        s = self.sems[sem_name]
        s[1] += op.k
        op.sem = sem_name
        op.val = s[1]
        self.ops.append(op)
        self.last_on[eng] = op
        return op

    def emit(self, name):
        nc = self.nc
        by_eng = {}
        for op in self.ops:
            by_eng.setdefault(op.eng, []).append(op)
        with nc.Block(name) as blk:
            for eng_name, ops in by_eng.items():
                def body(eng, ops=ops, eng_name=eng_name):
                    marks = self.marks[eng_name]
                    for op in ops:
                        for w in op.waits:
                            if marks.get(w.sem, 0) < w.val:
                                eng.wait_ge(self.sems[w.sem][0], w.val)
                                marks[w.sem] = w.val
                        instr = op.build(eng)
                        instr.then_inc(self.sems[op.sem][0], op.k)
                getattr(blk, eng_name)(body)
            def drainer(eng):
                marks = self.marks["sync"]
                for sn, (h, n) in self.sems.items():
                    if n > 0 and marks.get(sn, 0) < n:
                        eng.wait_ge(h, n)
                        marks[sn] = n
            blk.sync(drainer)
        for e, marks in self.marks.items():
            for sn, (h, n) in self.sems.items():
                marks[sn] = max(marks.get(sn, 0), n)
        self.ops = None

    def build(self):
        if "p0" in self.phases:
            self.p0()
        if "p1" in self.phases:
            self.p1()
        if "p2" in self.phases:
            self.p2()
        if self.dbg in ("kv", "q"):
            self.dbg_dump()
            return self.nc
        if "p3" in self.phases:
            self.p3()
            if self.dbg == "attn":
                return self.nc
        if "p4" in self.phases:
            self.p4()
            if self.dbg == "x2":
                return self.nc
        if "p56" in self.phases:
            self.p56()
        return self.nc

    def p0(self):
        nc = self.nc
        with contextlib.ExitStack() as stk:

            self.new_phase()
            add = self.add
            o_id = add("sync", lambda e: e.dma_start(self.ident_s[:, :], self.identd[:, :]), dma="id")

            consts = [
                add("vector", lambda v: v.memset(self.ones_c[:, :], 1.0)),
                add("vector", lambda v: v.memset(self.nones_c[:, :], -1.0)),
                add("vector", lambda v: v.memset(self.ones_rf[:, :], 1.0)),
                add("vector", lambda v: v.memset(self.ones_cf[:, :], 1.0)),
            ]
            self.emit("p0")


    def _wrows(self, add, wbf, wdram, fdim, dst, ps_ws):
        o_w = add("gpsimd", lambda g: g.dma_start(
            wbf[:, :, :], wdram.ap().rearrange("(m p) f -> p m f", p=P)), dma="wrow")
        prev_ev = None
        for c0 in range(0, fdim, 512):
            cw = min(512, fdim - c0)
            def mmws(t, c0=c0, cw=cw):
                for m in range(16):
                    ins = t.matmul(ps_ws[0:1, 0:cw], self.ones_c[:, :],
                                   wbf[:, m, c0:c0 + cw], start=(m == 0), stop=(m == 15))
                return ins
            o_mm = add("tensor", mmws, waits=[o_w, prev_ev])
            prev_ev = add("vector", lambda v, c0=c0, cw=cw: v.tensor_scalar_mul(
                dst[0:1, c0:c0 + cw], ps_ws[0:1, 0:cw], float(1.0 / D)), waits=[o_mm])
        return o_w, prev_ev
    def _stats_chunk(self, add, ctx, xbf_v, xsq_v, deps_x, deps_sq,
                     war_st, war_sm, war_negrow, war_strip):
        strip0, strip1, negrow, stt, r2h, ps_st, ps_sm = (
            ctx["strip0"], ctx["strip1"], ctx["negrow"], ctx["stt"], ctx["r2h"],
            ctx["ps_st"], ctx["ps_sm"])
        def mm_neg(t):
            for m in range(16):
                ins = t.matmul(ps_st[0:1, :], self.nones_c[:, :], xbf_v[:, m, :],
                               start=(m == 0), stop=(m == 15))
            return ins
        o_neg = add("tensor", mm_neg, waits=deps_x + war_st)
        def mm_s2(t):
            for m in range(16):
                ins = t.matmul(ps_st[32:33, :], self.ones_c[:, :], xsq_v[:, m, :],
                               start=(m == 0), stop=(m == 15))
            return ins
        o_s2 = add("tensor", mm_s2, waits=deps_sq + war_st)
        o_st0 = add("vector", lambda v: v.tensor_copy(strip0[0:1, :], ps_st[0:1, :]),
                    waits=[o_neg] + war_strip)
        o_st1 = add("vector", lambda v: v.tensor_copy(strip1[0:1, :], ps_st[32:33, :]), waits=[o_s2])
        o_nr = add("scalar", lambda e: e.copy(negrow[0:1, :], ps_st[0:1, :]),
                   waits=[o_neg, o_s2] + war_negrow)
        tts = []
        for tt in range(4):
            tts.append(add("tensor", lambda t, tt=tt: t.transpose(
                ps_sm[:, tt * 2:tt * 2 + 1],
                strip0[0:1, tt * 128:(tt + 1) * 128], self.ident_s[0:1, 0:1]),
                waits=[o_st0, o_st1] + war_sm))
            tts.append(add("tensor", lambda t, tt=tt: t.transpose(
                ps_sm[:, tt * 2 + 1:tt * 2 + 2],
                strip1[0:1, tt * 128:(tt + 1) * 128], self.ident_s[0:1, 0:1])))
        sview = ps_sm[:, 0:8].rearrange("p (tt two) -> p tt two", two=2)
        negc = sview[:, :, 0:1].squeeze(2)
        s2c = sview[:, :, 1:2].squeeze(2)
        o_c1 = add("vector", lambda v: v.tensor_copy(stt[:, 4:8], negc), waits=tts)
        o_c2 = add("vector", lambda v: v.tensor_scalar(
            stt[:, 0:4], s2c, float(1.0 / D), float(EPS), ALU.mult, ALU.add))
        add("vector", lambda v: v.tensor_mul(stt[:, 4:8], stt[:, 4:8], stt[:, 4:8]))
        add("vector", lambda v: v.scalar_tensor_tensor(
            stt[:, 0:4], stt[:, 4:8], float(-1.0 / D / D), stt[:, 0:4], ALU.mult, ALU.add))
        add("vector", lambda v: v.reciprocal(stt[:, 4:8], stt[:, 0:4]))
        o_r2 = add("vector", lambda v: v.tensor_scalar_mul(r2h[:, 0:4], stt[:, 4:8], float(0.5)))
        return o_nr, o_r2, (o_neg, o_s2)

    def _proj_build(self, add, hist, c, xbf, wbf, wsrow, negrow, ps_ab, abkv_f,
                    Ms, nM, o_castlast, o_nr, war_form):
        projs = []
        evs = []
        allev = hist[c - 1]["abev_all"] if c >= 1 else []
        for mc in range(nM):
            mw = Ms[mc]
            c0 = mc * 128
            wws = [o_castlast, o_nr]
            gi = c * nM + mc
            if gi >= 2:
                if mc >= 2:
                    wws.append(evs[mc - 2])
                else:
                    wws.append(allev[nM - 2 + mc])
            def mmproj(t, c=c, mc=mc, mw=mw, c0=c0, gi=gi):
                for m in range(16):
                    t.matmul(ps_ab[0:mw, gi % 2, :], wbf[:, m, c0:c0 + mw],
                             xbf[:, c % 2, m, :], start=(m == 0), stop=False)
                return t.matmul(ps_ab[0:mw, gi % 2, :], wsrow[0:1, c0:c0 + mw],
                                negrow[0:1, :], start=False, stop=True)
            projs.append(add("tensor", mmproj, waits=wws))
            evs.append(add("scalar", lambda e, c=c, mc=mc, mw=mw, gi=gi: e.copy(
                abkv_f[0:mw, c % 2, mc, :], ps_ab[0:mw, gi % 2, :]),
                waits=[projs[mc]] + war_form))
        return projs, evs

    def p1(self):
        nc = self.nc
        with contextlib.ExitStack() as stk:
            E = lambda *a: stk.enter_context(nc.sbuf_tensor(*a))
            wkv_bf = E("wkv_bf", [P, 16, FKV], BF16)
            wskv_r = E("wskv_r", [1, FKV], BF16)
            ps_ws = stk.enter_context(nc.psum_tensor("ws_ps1", [1, 512], F32))
            xstage = E("xstage", [P, 2, 16, 128], F32)
            xbf = E("xbf", [P, 2, 16, 512], BF16)
            xsq = E("xsq", [P, 16, 512], BF16)
            strip0 = E("strip0", [1, 512], F32)
            strip1 = E("strip1", [1, 512], F32)
            negrow = E("negrow", [1, 512], BF16)
            stt = E("stt", [P, 8], F32)
            r2h = E("r2h", [P, 4], F32)
            abkv_f = E("abkv_f", [P, 2, 5, 512], F32)
            abkv_tm = E("abkv_tm", [P, 2, 4, 576], F32)
            rott = E("rott", [P, 4, 4, 64], F32)
            costile = E("costile", [P, 2, 4, 64], F32)
            sintile = E("sintile", [P, 2, 4, 64], F32)
            ftmp1 = E("ftmp1", [P, 2048], F32)
            ftmp2 = E("ftmp2", [P, 2048], F32)
            vtmp = E("vtmp", [P, 2048], BF16)
            k_tm = E("k_tm", [P, 4, 2048], BF16)
            v_tm = E("v_tm", [P, 4, 2048], BF16)
            ps_st = stk.enter_context(nc.psum_tensor("ps_st", [P, 512], F32))
            ps_sm = stk.enter_context(nc.psum_tensor("ps_sm", [P, 512], F32))
            ps_ab = stk.enter_context(nc.psum_tensor("ps_ab", [P, 2, 512], F32))
            ps_at = stk.enter_context(nc.psum_tensor("ps_at", [P, 2, 512], F32))
            ps_t2 = stk.enter_context(nc.psum_tensor("ps_t2", [P, 512], F32))
            ctx = dict(strip0=strip0, strip1=strip1, negrow=negrow, stt=stt, r2h=r2h, ps_st=ps_st, ps_sm=ps_sm)

            self.new_phase()
            add = self.add
            o_wkv, o_wsrow = self._wrows(add, wkv_bf, self.wkvd, FKV, wskv_r, ps_ws)
            hist = []
            for c in range(NKC):
                H = {}
                hist.append(H)
                H["cast"] = []
                for qq in range(4):
                    gi = 4 * c + qq
                    wws = []
                    if gi >= 2:
                        cc, q2 = divmod(gi - 2, 4)
                        wws.append(hist[cc]["cast"][q2])
                    if qq == 0 and c >= 2:
                        wws += [hist[c - 2]["projlast"], hist[c - 2]["sq"]]
                    ld = add("sync", lambda e, c=c, qq=qq, gi=gi: e.dma_start(
                        xstage[:, gi % 2, :, :],
                        self.xk.ap().rearrange("(m p) t -> p m t", p=P)[
                            :, :, c * 512 + qq * 128:c * 512 + (qq + 1) * 128]),
                        waits=wws, dma="xs%d" % (gi % 2))
                    H["cast"].append(add("scalar", lambda e, c=c, qq=qq, gi=gi: e.copy(
                        xbf[:, c % 2, :, qq * 128:(qq + 1) * 128],
                        xstage[:, gi % 2, :, :]), waits=[ld]))
                sq_w = [H["cast"][3]] + ([hist[c - 1]["stats_s2"]] if c >= 1 else [])
                H["sq"] = add("scalar", lambda e, c=c: e.activation(
                    xsq[:, :, :], xbf[:, c % 2, :, :], AF.Square), waits=sq_w)
                war_st = [hist[c - 1]["strip_done"]] if c >= 1 else []
                war_sm = [hist[c - 1]["r2last"]] if c >= 1 else []
                war_nr = [hist[c - 1]["projlast"]] if c >= 1 else []
                war_strip = [hist[c - 1]["stT_done"]] if c >= 1 else []
                o_nr, o_r2, (o_negmm, o_s2mm) = self._stats_chunk(
                    add, ctx, xbf[:, c % 2, :, :], xsq[:, :, :],
                    [H["cast"][3]], [H["sq"]], war_st, war_sm, war_nr, war_strip)
                H["negrow"] = o_nr
                H["r2last"] = o_r2
                H["stats_s2"] = o_s2mm
                H["strip_done"] = o_nr
                war_form = [hist[c - 2]["formlast"]] if c >= 2 else []
                projs, evs = self._proj_build(add, hist, c, xbf, wkv_bf, wskv_r,
                                              negrow, ps_ab, abkv_f, FKVM, 5,
                                              H["cast"][3], o_nr, war_form)
                if c == 0:
                    projs[0].waits.append(o_wsrow)
                H["proj"] = projs
                H["projlast"] = projs[4]
                H["abev"] = evs
                H["abev_all"] = evs
                cs_w = [hist[c - 2]["rotlast"]] if c >= 2 else []
                o_cos = add("sync", lambda e, c=c: e.dma_start(
                    costile[:, c % 2, :, :],
                    self.cosb.ap().rearrange("(c tt p) w -> c p tt w", p=P, tt=4)[c]),
                    waits=cs_w, dma="cos%d" % (c % 2))
                o_sin = add("sync", lambda e, c=c: e.dma_start(
                    sintile[:, c % 2, :, :],
                    self.sinb.ap().rearrange("(c tt p) w -> c p tt w", p=P, tt=4)[c]),
                    dma="sin%d" % (c % 2))
                H["abTev"] = []
                H["ksp"] = []
                H["vsp"] = []
                formops = []
                vformops = []
                stT_done = None
                for tt in range(4):
                    gt = c * 4 + tt
                    tws = list(H["abev"])
                    if gt >= 2:
                        pc, pt = divmod(gt - 2, 4)
                        tws.append(hist[pc]["abTev"][pt])
                    tset = []
                    for mc in range(5):
                        mw = FKVM[mc]
                        if mc < 4:
                            dst = ps_at[:, gt % 2, mc * 128:(mc + 1) * 128]
                            extra = []
                        else:
                            dst = ps_t2[:, tt * 64:(tt + 1) * 64]
                            extra = ([H["abTev"][tt - 1]] if tt >= 1 else
                                     ([hist[c - 1]["abTev"][3]] if c >= 1 else []))
                        tset.append(add("tensor", lambda t, c=c, mc=mc, mw=mw, tt=tt, dst=dst: t.transpose(
                            dst, abkv_f[0:mw, c % 2, mc, tt * 128:(tt + 1) * 128],
                            self.ident_s[0:mw, 0:mw]),
                            waits=(tws if mc == 0 else [H["abev"][mc]]) + extra))
                    stT_done = tset[4]
                    ev_w = tset + ([hist[c - 2]["formlast"], hist[c - 2]["vformlast"]] if c >= 2 else [])
                    e1 = add("vector", lambda v, c=c, tt=tt: v.tensor_copy(
                        abkv_tm[:, c % 2, tt, 0:512], ps_at[:, (c * 4 + tt) % 2, :]), waits=ev_w)
                    e2 = add("vector", lambda v, c=c, tt=tt: v.tensor_copy(
                        abkv_tm[:, c % 2, tt, 512:576], ps_t2[:, tt * 64:(tt + 1) * 64]))
                    H["abTev"].append(e2)
                    bview = abkv_tm[:, c % 2, tt, :].rearrange("p (r ch) -> p r ch", ch=CH)
                    x1v, x2v = bview[:, :, 16:80], bview[:, :, 80:144]
                    cosv = costile[:, c % 2, tt, :].unsqueeze(1).broadcast_to([P, 4, 64])
                    sinv = sintile[:, c % 2, tt, :].unsqueeze(1).broadcast_to([P, 4, 64])
                    add("vector", lambda v, x1v=x1v, cosv=cosv: v.tensor_mul(rott[:, 0], x1v, cosv),
                        waits=[e1, e2, o_cos, o_sin])
                    add("vector", lambda v, x2v=x2v, sinv=sinv: v.tensor_mul(rott[:, 1], x2v, sinv))
                    add("vector", lambda v, x1v=x1v, sinv=sinv: v.tensor_mul(rott[:, 2], x1v, sinv))
                    add("vector", lambda v, x2v=x2v, cosv=cosv: v.tensor_mul(rott[:, 3], x2v, cosv))
                    add("vector", lambda v, x1v=x1v: v.tensor_add(x1v, rott[:, 0], rott[:, 1]))
                    H["rotlast"] = add("vector", lambda v, x2v=x2v: v.tensor_sub(x2v, rott[:, 3], rott[:, 2]))
                    av = bview[:, :, 0:16]
                    o_sc = add("vector", lambda v, av=av, tt=tt: v.tensor_scalar_mul(
                        av, av, r2h[:, tt:tt + 1]), waits=[o_r2])
                    spill_wait = [hist[c - 1]["ksp"][tt]] if c >= 1 else []
                    vspill_wait = [hist[c - 1]["vsp"][tt]] if c >= 1 else []
                    a0v = bview[:, 0, 0:16].unsqueeze(2).broadcast_to([P, 16, 128])
                    b0v = bview[:, 0, 16:144].unsqueeze(1).broadcast_to([P, 16, 128])
                    a1v = bview[:, 1, 0:16].unsqueeze(2).broadcast_to([P, 16, 128])
                    b1v = bview[:, 1, 16:144].unsqueeze(1).broadcast_to([P, 16, 128])
                    dv = k_tm[:, tt, :].rearrange("p (h dd) -> p h dd", h=16)
                    f1v = ftmp1[:, :].rearrange("p (h dd) -> p h dd", h=16)
                    f2v = ftmp2[:, :].rearrange("p (h dd) -> p h dd", h=16)
                    add("vector", lambda v, a0v=a0v, b0v=b0v, f1v=f1v: v.tensor_mul(f1v, a0v, b0v),
                        waits=[o_sc] + spill_wait)
                    add("vector", lambda v, a1v=a1v, b1v=b1v, f2v=f2v: v.tensor_mul(f2v, a1v, b1v))
                    formops.append(add("vector", lambda v, dv=dv, f1v=f1v, f2v=f2v: v.tensor_add(dv, f1v, f2v)))
                    a2v = bview[:, 2, 0:16].unsqueeze(2).broadcast_to([P, 16, 128])
                    b2v = bview[:, 2, 16:144].unsqueeze(1).broadcast_to([P, 16, 128])
                    a3v = bview[:, 3, 0:16].unsqueeze(2).broadcast_to([P, 16, 128])
                    b3v = bview[:, 3, 16:144].unsqueeze(1).broadcast_to([P, 16, 128])
                    vv = v_tm[:, tt, :].rearrange("p (h dd) -> p h dd", h=16)
                    vt = vtmp[:, :].rearrange("p (h dd) -> p h dd", h=16)
                    add("gpsimd", lambda g, a2v=a2v, b2v=b2v, vt=vt: g.tensor_mul(vt, a2v, b2v),
                        waits=[o_sc] + vspill_wait)
                    add("gpsimd", lambda g, a3v=a3v, b3v=b3v, vv=vv: g.tensor_mul(vv, a3v, b3v))
                    vformops.append(add("gpsimd", lambda g, vv=vv, vt=vt: g.tensor_add(vv, vv, vt)))
                    H["ksp"].append(add("gpsimd", lambda e, c=c, tt=tt: e.dma_start(
                        self.ksc.ap().rearrange("(c tt p) f -> c tt p f", p=P, tt=4)[c, tt],
                        k_tm[:, tt, :]), waits=[formops[-1]], dma="ksp%d" % tt))
                    H["vsp"].append(add("gpsimd", lambda e, c=c, tt=tt: e.dma_start(
                        self.vsc.ap().rearrange("(c tt p) f -> c tt p f", p=P, tt=4)[c, tt],
                        v_tm[:, tt, :]), waits=[vformops[-1]], dma="vsp%d" % tt))
                H["stT_done"] = stT_done
                H["formlast"] = formops[-1]
                H["vformlast"] = vformops[-1]
                H["spill"] = [H["ksp"][-1], H["vsp"][-1]]
            self.emit("p1")

    def p2(self):
        nc = self.nc
        with contextlib.ExitStack() as stk:
            E = lambda *a: stk.enter_context(nc.sbuf_tensor(*a))
            wq_bf = E("wq_bf", [P, 16, FQ], BF16)
            wsq_r = E("wsq_r", [1, FQ], BF16)
            wbuf = E("wbuf", [P, 2, 1366], F32)
            parts3 = E("parts3", [P, 4], F32)
            totr = E("totr", [1, 4], F32)
            srow = E("srow", [1, 8], F32)
            xstage = E("q_xstage", [P, 2, 16, 128], F32)
            xbf = E("q_xbf", [P, 2, 16, 512], BF16)
            xsq = E("q_xsq", [P, 16, 512], BF16)
            strip0 = E("q_strip0", [1, 512], F32)
            strip1 = E("q_strip1", [1, 512], F32)
            negrow = E("q_negrow", [1, 512], BF16)
            stt = E("q_stt", [P, 8], F32)
            r2h = E("q_r2h", [P, 4], F32)
            abq_f = E("abq_f", [P, 2, 7, 512], F32)
            abq_tm = E("abq_tm", [P, 2, 4, FQ], F32)
            ftmp1 = E("q_ftmp1", [P, 2048], F32)
            ftmp2 = E("q_ftmp2", [P, 2048], F32)
            qtmp1 = E("q_qtmp1", [P, 2048], F32)
            qtmp2 = E("q_qtmp2", [P, 2048], F32)
            q_tm = E("q_tm", [P, 2, 2048], BF16)
            ps_st = stk.enter_context(nc.psum_tensor("q_ps_st", [P, 512], F32))
            ps_sm = stk.enter_context(nc.psum_tensor("q_ps_sm", [P, 512], F32))
            ps_ab = stk.enter_context(nc.psum_tensor("q_ps_ab", [P, 2, 512], F32))
            ps_at = stk.enter_context(nc.psum_tensor("q_ps_at", [P, 2, 512], F32))
            ps_at2 = stk.enter_context(nc.psum_tensor("q_ps_at2", [P, 2, 512], F32))
            ctx = dict(strip0=strip0, strip1=strip1, negrow=negrow, stt=stt, r2h=r2h, ps_st=ps_st, ps_sm=ps_sm)

            self.new_phase()
            add = self.add
            o_wq, o_wsrow = self._wrows(add, wq_bf, self.wqd, FQ, wsq_r, ps_sm[0:1, 0:512])
            qspills = []
            o_pz = add("vector", lambda v: v.memset(self.parts[:, :], 0.0))
            widx = []
            for wi, wt in enumerate([self.w1t, self.w3t, self.w2t]):
                ncol = wt.shape[1]
                for ch in range(cdiv(wt.shape[0], P)):
                    for c0 in range(0, ncol, 1366):
                        widx.append((wi, ch, c0, min(1366, ncol - c0), wt))
            sstate = {"i": 0, "acc": {}}

            def pump_spass(n):
                for _ in range(n):
                    i = sstate["i"]
                    if i >= len(widx):
                        return
                    wi, ch, c0, cw, wt = widx[i]
                    r0, rw = ch * P, min(P, wt.shape[0] - ch * P)
                    sl = i % 2
                    ld = add("sync", lambda e, wt=wt, r0=r0, rw=rw, c0=c0, cw=cw, sl=sl:
                             e.dma_start(
                        wbuf[0:rw, sl, 0:cw], wt[r0:r0 + rw, c0:c0 + cw]),
                        waits=[sstate["acc"].get(i - 2)], dma="wb%d" % sl)
                    sstate["acc"][i] = add("scalar", lambda e, rw=rw, cw=cw, sl=sl, i=i:
                                           e.activation(
                        wbuf[0:rw, sl, 0:cw], wbuf[0:rw, sl, 0:cw], AF.Abs,
                        accum_out=self.parts[0:rw, i:i + 1]), waits=[ld, o_pz])
                    sstate["i"] = i + 1
            hist = []
            for c in range(NQC):
                H = {}
                hist.append(H)
                H["cast"] = []
                for qq in range(4):
                    gi = 4 * c + qq
                    wws = []
                    if gi >= 2:
                        cc, q2 = divmod(gi - 2, 4)
                        wws.append(hist[cc]["cast"][q2])
                    if qq == 0 and c >= 2:
                        wws += [hist[c - 2]["projlast"], hist[c - 2]["sq"]]
                    ld = add("sync", lambda e, c=c, qq=qq, gi=gi: e.dma_start(
                        xstage[:, gi % 2, :, :],
                        self.xqT.ap().rearrange("(m p) t -> p m t", p=P)[
                            :, :, c * 512 + qq * 128:c * 512 + (qq + 1) * 128]),
                        waits=wws, dma="qxs%d" % (gi % 2))
                    H["cast"].append(add("scalar", lambda e, c=c, qq=qq, gi=gi: e.copy(
                        xbf[:, c % 2, :, qq * 128:(qq + 1) * 128],
                        xstage[:, gi % 2, :, :]), waits=[ld]))
                sq_w = [H["cast"][3]] + ([hist[c - 1]["stats_s2"]] if c >= 1 else [])
                H["sq"] = add("scalar", lambda e, c=c: e.activation(
                    xsq[:, :, :], xbf[:, c % 2, :, :], AF.Square), waits=sq_w)
                war_st = [hist[c - 1]["strip_done"]] if c >= 1 else []
                war_sm = [hist[c - 1]["stT_rd"]] if c >= 1 else [o_wsrow]
                war_nr = [hist[c - 1]["projlast"]] if c >= 1 else []
                war_strip = [hist[c - 1]["stT_done"]] if c >= 1 else []
                o_nr, o_r2, (o_negmm, o_s2mm) = self._stats_chunk(
                    add, ctx, xbf[:, c % 2, :, :], xsq[:, :, :],
                    [H["cast"][3]], [H["sq"]], war_st, war_sm, war_nr, war_strip)
                o_r2b = add("vector", lambda v: v.tensor_scalar_mul(
                    r2h[:, 0:4], r2h[:, 0:4], float(1.0 / 3.0)))
                H["negrow"] = o_nr
                H["stats_s2"] = o_s2mm
                H["strip_done"] = o_nr
                H["stT_rd"] = o_r2
                war_form = [hist[c - 2]["formlast"]] if c >= 2 else []
                projs, evs = self._proj_build(add, hist, c, xbf, wq_bf, wsq_r,
                                              negrow, ps_ab, abq_f, FQM, 7,
                                              H["cast"][3], o_nr, war_form)
                if c == 0:
                    projs[0].waits.append(o_wsrow)
                H["proj"] = projs
                H["projlast"] = projs[6]
                H["abev"] = evs
                H["abev_all"] = evs
                H["abTev"] = []
                formops = []
                stT_done = None
                for tt in range(4):
                    gt = c * 4 + tt
                    tws = list(H["abev"])
                    if gt >= 2:
                        pc, pt = divmod(gt - 2, 4)
                        tws.append(hist[pc]["abTev"][pt])
                    tset = []
                    for mc in range(7):
                        mw = FQM[mc]
                        if mc < 4:
                            dst = ps_at[:, gt % 2, mc * 128:(mc + 1) * 128]
                        else:
                            dst = ps_at2[:, gt % 2, (mc - 4) * 128:(mc - 4) * 128 + mw]
                        tset.append(add("tensor", lambda t, c=c, mc=mc, mw=mw, tt=tt, dst=dst: t.transpose(
                            dst, abq_f[0:mw, c % 2, mc, tt * 128:(tt + 1) * 128],
                            self.ident_s[0:mw, 0:mw]),
                            waits=(tws if mc == 0 else [H["abev"][mc]])))
                    stT_done = tset[-1]
                    ev_w = tset + ([hist[c - 2]["formlast"]] if c >= 2 else [])
                    e1 = add("vector", lambda v, c=c, tt=tt: v.tensor_copy(
                        abq_tm[:, c % 2, tt, 0:512], ps_at[:, (c * 4 + tt) % 2, :]), waits=ev_w)
                    e2 = add("vector", lambda v, c=c, tt=tt: v.tensor_copy(
                        abq_tm[:, c % 2, tt, 512:864], ps_at2[:, (c * 4 + tt) % 2, 0:352]))
                    H["abTev"].append(e2)
                    bview = abq_tm[:, c % 2, tt, :].rearrange("p (r ch) -> p r ch", ch=CH)
                    av = bview[:, :, 0:16]
                    o_sc = add("vector", lambda v, av=av, tt=tt: v.tensor_scalar_mul(
                        av, av, r2h[:, tt:tt + 1]), waits=[o_r2b, e1, e2])
                    spill_wait = []
                    dv = q_tm[:, tt % 2, :].rearrange("p (h dd) -> p h dd", h=16)
                    f1v = ftmp1[:, :].rearrange("p (h dd) -> p h dd", h=16)
                    f2v = ftmp2[:, :].rearrange("p (h dd) -> p h dd", h=16)
                    t1v = qtmp1[:, :].rearrange("p (h dd) -> p h dd", h=16)
                    t2v = qtmp2[:, :].rearrange("p (h dd) -> p h dd", h=16)
                    def bc(r, ab):
                        if ab == 0:
                            return bview[:, r, 0:16].unsqueeze(2).broadcast_to([P, 16, 128])
                        return bview[:, r, 16:144].unsqueeze(1).broadcast_to([P, 16, 128])
                    add("vector", lambda v, a=bc(0, 0), b=bc(0, 1), f1v=f1v: v.tensor_mul(f1v, a, b),
                        waits=[o_sc] + spill_wait)
                    add("vector", lambda v, a=bc(1, 0), b=bc(1, 1), f2v=f2v: v.tensor_mul(f2v, a, b))
                    add("vector", lambda v, f1v=f1v, f2v=f2v: v.tensor_add(f1v, f1v, f2v))
                    add("vector", lambda v, a=bc(2, 0), b=bc(2, 1), f2v=f2v: v.tensor_mul(f2v, a, b))
                    o_dvp = add("vector", lambda v, f1v=f1v, f2v=f2v: v.tensor_add(f1v, f1v, f2v))
                    add("gpsimd", lambda g, a=bc(3, 0), b=bc(3, 1), t1v=t1v: g.tensor_mul(t1v, a, b),
                        waits=[o_sc] + spill_wait + (formops[-1:] if formops else []))
                    add("gpsimd", lambda g, a=bc(4, 0), b=bc(4, 1), t2v=t2v: g.tensor_mul(t2v, a, b))
                    add("gpsimd", lambda g, t1v=t1v, t2v=t2v: g.tensor_add(t1v, t1v, t2v))
                    add("gpsimd", lambda g, a=bc(5, 0), b=bc(5, 1), t2v=t2v: g.tensor_mul(t2v, a, b))
                    o_plp = add("gpsimd", lambda g, t1v=t1v, t2v=t2v: g.tensor_add(t1v, t1v, t2v))
                    formops.append(add("vector", lambda v, dv=dv, f1v=f1v, t1v=t1v: v.tensor_add(dv, f1v, t1v),
                                       waits=[o_plp] + ([qspills[gt - 2]] if gt >= 2 else [])))
                    qspills.append(add("gpsimd", lambda e, c=c, tt=tt: e.dma_start(
                        self.qsc.ap().rearrange("(c tt p) f -> c tt p f", p=P, tt=4)[c, tt],
                        q_tm[:, tt % 2, :]), waits=[formops[-1]], dma="qsp%d" % (tt % 2)))
                    pump_spass(14)
                H["stT_done"] = stT_done
                H["formlast"] = formops[-1]
                H["spill"] = [qspills[-1]]
            pump_spass(len(widx))
            rng_w = {}
            pos = 0
            for wi, wt in enumerate([self.w1t, self.w3t, self.w2t]):
                n = cdiv(wt.shape[0], P) * cdiv(wt.shape[1], 1366)
                rng_w[wi] = (pos, pos + n)
                pos += n
            p3f = []
            for wi in range(3):
                lo, hi = rng_w[wi]
                p3f.append(add("vector", lambda v, wi=wi, lo=lo, hi=hi: v.tensor_reduce(
                    parts3[:, wi:wi + 1], self.parts[:, lo:hi], AX.X, ALU.add),
                    waits=[sstate["acc"][len(widx) - 1]]))
            o_z3 = add("vector", lambda v: v.memset(parts3[:, 3:4], 0.0))
            sm_guard = [hist[NQC - 1]["stT_rd"], hist[NQC - 1]["abTev"][3]]
            o_totmm = add("tensor", lambda t: t.matmul(
                ps_sm[0:4, 8:9], parts3[:, 0:4], self.ones_cf[:, :], start=True, stop=True),
                waits=p3f + [o_z3] + sm_guard)
            o_totcp = add("vector", lambda v: v.tensor_copy(parts3[0:4, 3:4], ps_sm[0:4, 8:9]),
                          waits=[o_totmm])
            o_totT = add("tensor", lambda t: t.transpose(
                ps_sm[0:1, 12:16], parts3[0:4, 3:4], self.ident_s[0:4, 0:4]),
                waits=[o_totcp])
            o_totr = add("vector", lambda v: v.tensor_copy(totr[0:1, 0:4], ps_sm[0:1, 12:16]),
                         waits=[o_totT])
            cpos = float(0.5 / NW)
            cneg = float(-0.5 / NW)
            sr = [add("vector", lambda v: v.tensor_scalar_mul(srow[0:1, 0:1], totr[0:1, 0:1], cpos), waits=[o_totr])]
            sr.append(add("vector", lambda v: v.tensor_scalar_mul(srow[0:1, 1:2], totr[0:1, 0:1], cneg)))
            sr.append(add("vector", lambda v: v.tensor_scalar_mul(srow[0:1, 2:3], totr[0:1, 1:2], cpos)))
            sr.append(add("vector", lambda v: v.tensor_scalar_mul(srow[0:1, 3:4], totr[0:1, 1:2], cneg)))
            sr.append(add("vector", lambda v: v.tensor_scalar_mul(srow[0:1, 4:5], totr[0:1, 2:3], cpos)))
            sr.append(add("vector", lambda v: v.tensor_scalar_mul(srow[0:1, 5:6], totr[0:1, 2:3], cneg)))
            sr.append(add("vector", lambda v: v.tensor_scalar_mul(srow[0:1, 6:7], totr[0:1, 0:1], float(1.0 / NW))))
            sr.append(add("vector", lambda v: v.tensor_mul(srow[0:1, 7:8], totr[0:1, 1:2], totr[0:1, 2:3])))
            sr.append(add("vector", lambda v: v.tensor_scalar_mul(srow[0:1, 7:8], srow[0:1, 7:8], float(1.0 / NW / NW))))
            o_bc0 = add("tensor", lambda t: t.matmul(
                ps_sm[:, 16:24], self.ones_rf[:, :], srow[0:1, 0:8], start=True, stop=True),
                waits=sr)
            add("vector", lambda v: v.tensor_copy(self.scal[:, 0:8], ps_sm[:, 16:24]), waits=[o_bc0])
            self.emit("p2")

    def dbg_dump(self):
        self.new_phase()
        add = self.add
        if self.dbg == "kv":
            add("sync", lambda e: e.dma_start(self.okv_k[:, :], self.ksc[:, :]), dma="dbg")
            add("sync", lambda e: e.dma_start(self.okv_v[:, :], self.vsc[:, :]), dma="dbg")
        elif self.dbg == "q":
            add("sync", lambda e: e.dma_start(self.oq[:, :], self.qsc[:, :]), dma="dbg")
        nc = self.nc
        with nc.sbuf_tensor("zb", [P, D], F32) as zb:
            zz = add("vector", lambda v: v.memset(zb[:, :], 0.0))
            for i in range(QL // P):
                add("sync", lambda e, i=i: e.dma_start(self.yd[i * P:(i + 1) * P, :], zb[:, :]),
                    waits=[zz], dma="dbg")
            self.emit("dbg")

    def p3(self):
        nc = self.nc
        self._stk_attn = contextlib.ExitStack()
        self.attn_y = self._stk_attn.enter_context(nc.sbuf_tensor("attn_y", [P, 16, QL], BF16))
        with contextlib.ExitStack() as stk:
            E = lambda *a: stk.enter_context(nc.sbuf_tensor(*a))
            Kh = E("Kh", [P, 2, T], BF16)
            Vh = E("Vh", [P, 2, 64, 128], BF16)
            Qh = E("Qh", [P, 2, QL], BF16)
            Pb = E("Pb", [P, 16, 512], BF16)
            mask_s = E("mask_s", [P, 16, 512], BF16)
            rl_f = E("rl_f", [1, 512], F32)
            rlb = E("rlb", [P, 2, 512], F32)
            wstg = E("wstg", [P, 4, 8, 128], F32)
            wqst = E("wqst", [P, 2, 8, 128], BF16)
            qscr = E("qscr", [P, 8, 128], BF16)
            ps_s = stk.enter_context(nc.psum_tensor("ps_s", [P, 2, 1024], F32))
            ps_o = stk.enter_context(nc.psum_tensor("ps_o", [P, 2, 512], F32))
            ps_l = stk.enter_context(nc.psum_tensor("ps_l", [P, 512], F32))
            ps_rb = stk.enter_context(nc.psum_tensor("ps_rb", [P, 512], F32))

            self.new_phase()
            add = self.add
            o_mask = add("sync", lambda e: e.dma_start(
                mask_s[:, :, :], self.dmaskd.ap().rearrange("kl i j -> i kl j")), dma="mask")

            o_wz = add("vector", lambda v: v.memset(wstg[:, :, :, :], 0.0))
            qtasks = []
            for wi, (wt, wsc) in enumerate(((self.w1t, self.wq1sc), (self.w3t, self.wq3sc))):
                for fc in range(NFC):
                    fw = 128 if fc < NFC - 1 else FC_LAST
                    for half in (0, 1):
                        qtasks.append(("a", wi, wt, wsc, fc, fw, half))
            for dc in range(16):
                for f0 in range(0, NFC, 8):
                    qtasks.append(("b", dc, f0, min(8, NFC - f0)))
            qstate = {"i": 0, "q2": {}, "sp": {}}

            def pump_quant(n):
                for _ in range(n):
                    i = qstate["i"]
                    if i >= len(qtasks):
                        return
                    task = qtasks[i]
                    par = i % 4
                    spar = i % 2
                    ldw = [qstate["q2"].get(i - 4, o_wz)]
                    if task[0] == "a":
                        _, wi, wt, wsc, fc, fw, half = task
                        nc_ = 8 * 128
                        lds = [add("sync", lambda e, wt=wt, fc=fc, fw=fw, half=half, par=par:
                                   e.dma_start(
                            wstg[:, par, :, 0:fw],
                            wt.ap().rearrange("(m p) h -> p m h", p=P)[
                                :, half * 8:half * 8 + 8, fc * 128:fc * 128 + fw]),
                            waits=ldw, dma="qw%d" % par)]
                        pc, ngc = 2 * wi, 2 * wi + 1
                        dst = lambda e, wsc=wsc, fc=fc, half=half, spar=spar: e.dma_start(
                            wsc[fc][:, half * 1024:(half + 1) * 1024], wqst[:, spar, :, :])
                    else:
                        _, dc, f0, fn = task
                        nfull = fn if f0 + fn <= NFC - 1 else fn - 1
                        nc_ = fn * 128
                        lds = [add("sync", lambda e, dc=dc, f0=f0, nfull=nfull, par=par:
                                   e.dma_start(
                            wstg[:, par, 0:nfull, :],
                            self.w2t.ap()[0:(NFC - 1) * P, :].rearrange(
                                "(fc p) d -> p fc d", p=P)[:, f0:f0 + nfull,
                                dc * 128:(dc + 1) * 128]),
                            waits=ldw, dma="qw%d" % par)]
                        if nfull < fn:
                            lds.append(add("sync", lambda e, dc=dc, nfull=nfull, par=par:
                                           e.dma_start(
                                wstg[0:FC_LAST, par, nfull, :],
                                self.w2t[(NFC - 1) * P:HID, dc * 128:(dc + 1) * 128]),
                                waits=ldw, dma="qwt"))
                        pc, ngc = 4, 5
                        dst = lambda e, dc=dc, f0=f0, fn=fn, spar=spar: e.dma_start(
                            self.wq2sc[dc][:, f0 * 128:(f0 + fn) * 128],
                            wqst[:, spar, 0:fn, :])
                    add("vector", lambda v, par=par, nc_=nc_, ngc=ngc: v.tensor_scalar(
                        qscr.ap().rearrange("p a b -> p (a b)")[:, 0:nc_],
                        wstg.ap().rearrange("p s a b -> p s (a b)")[:, par, 0:nc_],
                        self.scal[:, ngc:ngc + 1], -1.0, ALU.is_gt, ALU.add),
                        waits=lds + [qstate["sp"].get(i - 2)])
                    o_q2 = add("vector", lambda v, par=par, spar=spar, nc_=nc_, pc=pc: v.scalar_tensor_tensor(
                        wqst.ap().rearrange("p s a b -> p s (a b)")[:, spar, 0:nc_],
                        wstg.ap().rearrange("p s a b -> p s (a b)")[:, par, 0:nc_],
                        self.scal[:, pc:pc + 1],
                        qscr.ap().rearrange("p a b -> p (a b)")[:, 0:nc_],
                        ALU.is_gt, ALU.add))
                    qstate["q2"][i] = o_q2
                    qstate["sp"][i] = add("sync", dst, waits=[o_q2], dma="qs%d" % spar)
                    qstate["i"] = i + 1

            exp_hist = []
            pv_cover = []
            l_cover = []
            prev_oev = {0: None, 1: None}
            prev_rl = {0: None, 1: None}
            prev_rbe = None
            prev_orb = [None]
            hist = []
            gi = 0
            for h in range(NH):
                H = {}
                hist.append(H)
                lw = [hist[h - 2]["hdone"]] if h >= 2 else []
                o_K = add("sync", lambda e, h=h: e.dma_start(
                    Kh[:, h % 2, :], self.ksc[:, h * 128:(h + 1) * 128], transpose=True),
                    waits=lw, dma="K%d" % (h % 2))
                o_V = add("sync", lambda e, h=h: e.dma_start(
                    Vh[:, h % 2, :, :],
                    self.vsc.ap().rearrange("(kt p) f -> p kt f", p=P)[:, :, h * 128:(h + 1) * 128]),
                    waits=lw, dma="V%d" % (h % 2))
                o_Q = add("sync", lambda e, h=h: e.dma_start(
                    Qh[:, h % 2, :], self.qsc[:, h * 128:(h + 1) * 128], transpose=True),
                    waits=lw, dma="Q%d" % (h % 2))
                qcd = []
                for qc in range(NQC):
                    nkt = 16 * (qc + 1)
                    o_pv = None
                    o_l = None
                    def flush(pend, qc=qc, nkt=nkt):
                        (kt, ib, pos, o_e) = pend
                        grp_start = (pos == 0)
                        grp_stop = (pos == nkt - 1)
                        pw = [o_e, o_V]
                        if grp_start:
                            pw.append(prev_oev[qc % 2])
                        o_pv = add("tensor", lambda t, h=h, qc=qc, kt=kt, ib=ib,
                                   grp_start=grp_start, grp_stop=grp_stop: t.matmul(
                            ps_o[:, qc % 2, :], Vh[:, h % 2, kt, :], Pb[:, ib % 16, :],
                            start=grp_start, stop=grp_stop), waits=pw)
                        lwz = [o_e]
                        if grp_start:
                            lwz.append(prev_rl[0])
                        o_l = add("tensor", lambda t, qc=qc, ib=ib,
                                  grp_start=grp_start, grp_stop=grp_stop: t.matmul(
                            ps_l[0:1, :], self.ones_c[:, :], Pb[:, ib % 16, :],
                            start=grp_start, stop=grp_stop), waits=lwz)
                        pv_cover.append(o_pv)
                        l_cover.append(o_l)
                        return o_pv, o_l
                    pend = []
                    if qc == 0:
                        kt_order = list(range(nkt))
                    else:
                        kt_order = []
                        for i in range(16):
                            kt_order.extend(range(i * qc, (i + 1) * qc))
                            kt_order.append(16 * qc + i)
                    for pos, kt in enumerate(kt_order):
                        ib = gi
                        sws = [o_K, o_Q]
                        if ib >= 4:
                            sws.append(exp_hist[ib - 4])
                        o_s = add("tensor", lambda t, h=h, qc=qc, kt=kt, ib=ib: t.matmul(
                            ps_s[:, (ib // 2) % 2, (ib % 2) * 512:(ib % 2) * 512 + 512],
                            Kh[:, h % 2, kt * 128:(kt + 1) * 128],
                            Qh[:, h % 2, qc * 512:(qc + 1) * 512], start=True, stop=True),
                            waits=sws)
                        ews = [o_s]
                        if ib >= 16:
                            ews += [pv_cover[ib - 16], l_cover[ib - 16]]
                        o_ex = add("scalar", lambda e, ib=ib: e.activation(
                            Pb[:, ib % 16, :],
                            ps_s[:, (ib // 2) % 2, (ib % 2) * 512:(ib % 2) * 512 + 512],
                            AF.Exp, bias=f32(EXPB), scale=f32(SCALE)), waits=ews)
                        o_e = o_ex
                        if kt >= 16 * qc:
                            kl = kt - 16 * qc
                            meng = "vector" if qc == 0 else "gpsimd"
                            o_e = add(meng, lambda v, ib=ib, kl=kl: v.tensor_mul(
                                Pb[:, ib % 16, :], Pb[:, ib % 16, :], mask_s[:, kl, :]),
                                waits=[o_ex, o_mask])
                        exp_hist.append(o_ex)
                        pend.append((kt, ib, pos, o_e))
                        if pos == nkt // 2:
                            pump_quant(3)
                        if len(pend) > 2:
                            o_pv, o_l = flush(pend.pop(0))
                        gi += 1
                    for item in pend:
                        o_pv, o_l = flush(item)
                    o_rl = add("vector", lambda v, qc=qc: v.reciprocal(
                        rl_f[0:1, :], ps_l[0:1, :]),
                        waits=[o_l, prev_orb[0]])
                    prev_rl[0] = o_rl
                    o_rb = add("tensor", lambda t: t.matmul(
                        ps_rb[:, :], self.ones_rf[:, :], rl_f[0:1, :], start=True, stop=True),
                        waits=[o_rl, prev_rbe])
                    prev_orb[0] = o_rb
                    o_rbe = add("vector", lambda v, qc=qc: v.tensor_copy(
                        rlb[:, qc % 2, :], ps_rb[:, :]), waits=[o_rb])
                    prev_rbe = o_rbe
                    o_ev = add("vector", lambda v, h=h, qc=qc: v.tensor_mul(
                        self.attn_y[:, h, qc * 512:(qc + 1) * 512], ps_o[:, qc % 2, :],
                        rlb[:, qc % 2, :]), waits=[o_pv, o_rbe])
                    prev_oev[qc % 2] = o_ev
                    qcd.append(o_ev)
                    pump_quant(2)
                H["hdone"] = qcd[-1]
            assert len(pv_cover) == gi and len(l_cover) == gi
            pump_quant(len(qtasks))
            if self.dbg == "attn":
                add("sync", lambda e: e.dma_start(
                    self.oattn.ap().rearrange("(m p) j -> p m j", p=P), self.attn_y[:, :, :]),
                    waits=[hist[i]["hdone"] for i in range(NH)], dma="dbg")
            self.emit("p3")

    def _ln_tok(self, add, row_in, out_fn, sqt, mrc, deps):
        o_r1 = add("vector", lambda v: v.tensor_reduce(mrc[:, 0:1], row_in, AX.X, ALU.add),
                   waits=deps)
        o_sq = add("scalar", lambda e: e.activation(sqt[:, :], row_in, AF.Square), waits=deps)
        o_r2 = add("vector", lambda v: v.tensor_reduce(mrc[:, 1:2], sqt[:, :], AX.X, ALU.add),
                   waits=[o_sq])
        add("vector", lambda v: v.tensor_scalar_mul(mrc[:, 2:3], mrc[:, 0:1], float(1.0 / D)))
        add("vector", lambda v: v.tensor_mul(mrc[:, 3:4], mrc[:, 2:3], mrc[:, 2:3]))
        o_var = add("vector", lambda v: v.scalar_tensor_tensor(
            mrc[:, 4:5], mrc[:, 1:2], float(1.0 / D), mrc[:, 3:4], ALU.mult, ALU.subtract))
        o_ln = add("scalar", lambda e: e.activation(mrc[:, 5:6], mrc[:, 4:5], AF.Ln,
                                                    bias=f32(EPS)), waits=[o_var])
        o_rs = add("scalar", lambda e: e.activation(mrc[:, 6:7], mrc[:, 5:6], AF.Exp,
                                                    scale=f32(-0.5)))
        return out_fn(mrc[:, 2:3], mrc[:, 6:7], [o_rs])

    def p4(self):
        nc = self.nc
        with contextlib.ExitStack() as stk:
            E = lambda *a: stk.enter_context(nc.sbuf_tensor(*a))
            wob = E("wob", [P, 2, 16, 128], BF16)
            a2e = E("a2e", [P, 2, 512], F32)
            a2q = E("a2q", [P, 4, 2048], F32)
            xqt_s = E("xqt_s", [P, 2, 2048], F32)
            x2t = E("x2t", [P, 2, 2048], F32)
            ln3t = E("ln3t", [P, 2048], F32)
            sqt = E("sqt", [P, 2048], F32)
            mrc = E("mrc", [P, 8], F32)
            ps_a2 = stk.enter_context(nc.psum_tensor("ps_a2", [P, 2, 512], F32))
            ps_tT = stk.enter_context(nc.psum_tensor("ps_tT", [P, 2, 512], F32))
            ps_l3 = stk.enter_context(nc.psum_tensor("ps_l3", [P, 2, 512], F32))

            self.new_phase()
            add = self.add
            hist = []
            prev_wcast = [None]
            xq_hist = []
            xsp_hist = []
            hist_wo = []
            hist_a2ev = []
            hist_tTe = []
            hist_l3e = []
            hist_l3tp = []
            for qc in range(NQC):
                H = {"a2ev": []}
                hist.append(H)

                for dc in range(16):
                    gdc = qc * 16 + dc
                    o_wo = add("gpsimd", lambda e, dc=dc, gdc=gdc: e.dma_start(
                        wob[:, gdc % 2, :, :],
                        self.wod.ap().rearrange("(m p) d -> p m d", p=P)[
                            :, :, dc * 128:(dc + 1) * 128]),
                        waits=[hist_wo[gdc - 2]] if gdc >= 2 else [], dma="wo%d" % (gdc % 2))
                    def mm_a2(t, qc=qc, gdc=gdc):
                        for m in range(16):
                            ins = t.matmul(ps_a2[:, gdc % 2, :], wob[:, gdc % 2, m, :],
                                           self.attn_y[:, m, qc * 512:(qc + 1) * 512],
                                           start=(m == 0), stop=(m == 15))
                        return ins
                    mmw = [o_wo]
                    if gdc >= 2:
                        mmw.append(hist_a2ev[gdc - 2])
                    o_mm = add("tensor", mm_a2, waits=mmw)
                    hist_wo.append(o_mm)
                    o_ev = add("scalar", lambda e, gdc=gdc: e.copy(
                        a2e[:, gdc % 2, :], ps_a2[:, gdc % 2, :]),
                        waits=[o_mm] + ([hist_tTe[(gdc - 2) * 4 + 3]] if gdc >= 2 else []))
                    hist_a2ev.append(o_ev)
                    H["a2ev"].append(o_ev)
                    for tt in range(4):
                        gt = gdc * 4 + tt
                        tpw = [o_ev]
                        if gt >= 2:
                            tpw.append(hist_tTe[gt - 2])
                        o_tp = add("tensor", lambda t, tt=tt, gt=gt, gdc=gdc: t.transpose(
                            ps_tT[:, gt % 2, 0:128], a2e[:, gdc % 2, tt * 128:(tt + 1) * 128],
                            self.ident_s[:, :]), waits=tpw)
                        hist_tTe.append(add("vector", lambda v, tt=tt, dc=dc, gt=gt: v.tensor_copy(
                            a2q[:, tt, dc * 128:(dc + 1) * 128], ps_tT[:, gt % 2, 0:128]),
                            waits=[o_tp]))
                x2dones = []
                for tt in range(4):
                    gxq = qc * 4 + tt
                    o_xq = add("sync", lambda e, qc=qc, tt=tt, gxq=gxq: e.dma_start(
                        xqt_s[:, gxq % 2, :],
                        self.xqd[qc * 512 + tt * 128:qc * 512 + (tt + 1) * 128, :]),
                        waits=[xq_hist[gxq - 2]] if gxq >= 2 else [], dma="xq%d" % (gxq % 2))
                    deps = [hist_tTe[(qc * 16 + dc) * 4 + tt] for dc in range(16)]
                    row = a2q[:, tt, :]
                    def apply_ln2(mcol, rcol, w, row=row, tt=tt, qc=qc, gxq=gxq, o_xq=o_xq):
                        o_n = add("vector", lambda v: v.tensor_scalar(
                            sqt[:, :], row, mcol, rcol, ALU.subtract, ALU.mult), waits=w)
                        return add("vector", lambda v, tt=tt, gxq=gxq: v.tensor_add(
                            x2t[:, gxq % 2, :], xqt_s[:, gxq % 2, :], sqt[:, :]),
                            waits=[o_xq] + ([xsp_hist[gxq - 2]] if gxq >= 2 else []))
                    o_x2 = self._ln_tok(add, row, apply_ln2, sqt, mrc, deps)
                    xq_hist.append(o_x2)
                    H.setdefault("x2add", []).append(o_x2)
                    def apply_ln3(mcol, rcol, w, tt=tt, gxq=gxq):
                        return add("vector", lambda v, gxq=gxq: v.tensor_scalar(
                            ln3t[:, :], x2t[:, gxq % 2, :], mcol, rcol, ALU.subtract, ALU.mult),
                            waits=w + ([hist_l3tp[-1]] if hist_l3tp else []))
                    o_l3 = self._ln_tok(add, x2t[:, gxq % 2, :], apply_ln3, sqt, mrc, [o_x2])
                    for dc2 in range(16):
                        gl = (qc * 4 + tt) * 16 + dc2
                        tpw = [o_l3]
                        if gl >= 2:
                            tpw.append(hist_l3e[gl - 2])
                        o_tp = add("tensor", lambda t, dc2=dc2, gl=gl: t.transpose(
                            ps_l3[:, gl % 2, 0:128], ln3t[:, dc2 * 128:(dc2 + 1) * 128],
                            self.ident_s[:, :]), waits=tpw)
                        hist_l3tp.append(o_tp)
                        hist_l3e.append(add("vector", lambda v, tt=tt, dc2=dc2, qc=qc, gl=gl: v.tensor_copy(
                            self.attn_y[:, dc2, qc * 512 + tt * 128:qc * 512 + (tt + 1) * 128],
                            ps_l3[:, gl % 2, 0:128]), waits=[o_tp, hist_a2ev[qc * 16 + 15]]))
                    o_xsp = add("sync", lambda e, qc=qc, tt=tt, gxq=gxq: e.dma_start(
                        self.x2sc[qc * 512 + tt * 128:qc * 512 + (tt + 1) * 128, :],
                        x2t[:, gxq % 2, :]), waits=[o_l3], dma="x2s%d" % (gxq % 2))
                    xsp_hist.append(o_xsp)
                    x2dones.append(o_xsp)
                H["x2sp"] = x2dones
                H["x2done"] = x2dones[-1]
            add("sync", lambda e: e.dma_start(
                self.ln3sc.ap().rearrange("(m p) j -> p m j", p=P), self.attn_y[:, :, :]),
                waits=[hist_l3e[-1]], dma="l3sp")
            if self.dbg == "x2":
                add("sync", lambda e: e.dma_start(self.ox2[:, :], self.x2sc[:, :]),
                    waits=[t for i in range(NQC) for t in hist[i]["x2sp"]], dma="dbg")
            self.emit("p4")

    def p56(self):
        nc = self.nc
        self._stk_attn.close()
        with contextlib.ExitStack() as stk:
            E = lambda *a: stk.enter_context(nc.sbuf_tensor(*a))
            ln3q = E("ln3q", [P, 16, 512], BF16)
            w1q = E("w1q", [P, 2, 2048], BF16)
            w3q = E("w3q", [P, 2, 2048], BF16)
            gsb = E("gsb", [P, 2, NFC, 512], BF16)
            silu_t = E("silu_t", [P, 512], F32)
            w2q = E("w2q", [P, 4, 22, 128], BF16)
            h2f = E("h2f", [P, 512], F32)
            out_tm = E("out_tm", [P, 4, 2048], F32)
            x2r = E("x2r", [P, 2048], F32)
            sqt = E("sqt2", [P, 2048], F32)
            mrc = E("mrc2", [P, 8], F32)
            ps_h1 = stk.enter_context(nc.psum_tensor("ps_h1", [P, 2, 512], F32))
            ps_h3 = stk.enter_context(nc.psum_tensor("ps_h3", [P, 2, 512], F32))
            ps_h2 = stk.enter_context(nc.psum_tensor("ps_h2", [P, 2, 512], F32))
            ps_oT = stk.enter_context(nc.psum_tensor("ps_oT", [P, 2, 512], F32))

            self.new_phase()
            add = self.add
            prev_q = {}
            mm1_hist = {}
            glast = {}
            q_last_mm1 = {}
            o_l3 = {}
            prev_w2 = {}
            mm2_hist = {}
            hist_oev = []
            ydones = []
            ydones_q = {}
            prev_oy = []
            jobs = []

            def emit_mm2(tq, dc):
                gdc = tq * 16 + dc
                wq2toks = []
                pieces = [(0, 22), (22, 21)]
                for pi, (f0, fn) in enumerate(pieces):
                    gpi = gdc * 2 + pi
                    slot = (dc % 2) * 2 + pi
                    mmwar2 = [mm2_hist.get(gdc - 2)] if pi == 0 else []
                    o_wq2 = add("sync", lambda e, dc=dc, tq=tq, f0=f0, fn=fn, slot=slot: e.dma_start(
                        w2q[:, slot, 0:fn, :],
                        self.wq2sc[dc][:, f0 * 128:(f0 + fn) * 128]),
                        waits=[prev_w2.get(gpi - 4)] + mmwar2, dma="w2_%d" % slot)
                    prev_w2[gpi] = o_wq2
                    wq2toks.append(o_wq2)
                def mmh2(t, tq=tq, dc=dc):
                    for pi, (f0, fn) in enumerate(pieces):
                        slot = (dc % 2) * 2 + pi
                        for fi in range(fn):
                            fc = f0 + fi
                            fw = 128 if fc < NFC - 1 else FC_LAST
                            ins = t.matmul(ps_h2[:, gdc % 2, :],
                                           w2q[0:fw, slot, fi, :],
                                           gsb[0:fw, tq % 2, fc, :],
                                           start=(fc == 0), stop=(fc == NFC - 1))
                    return ins
                mmw = list(wq2toks) + [glast[tq * NFC + NFC - 1]]
                if gdc >= 2 and hist_oev:
                    mmw.append(hist_oev[-1])
                o_mm2 = add("tensor", mmh2, waits=mmw)
                mm2_hist[gdc] = o_mm2
                o_h2e = add("vector", lambda v, gdc=gdc: v.tensor_scalar_mul(
                    h2f[:, :], ps_h2[:, gdc % 2, :], self.scal[:, 7:8]), waits=[o_mm2])
                for tt in range(4):
                    gt = gdc * 4 + tt
                    tpw = [o_h2e]
                    if len(hist_oev) >= 2:
                        tpw.append(hist_oev[-2])
                    o_tp = add("tensor", lambda t, tt=tt, gt=gt: t.transpose(
                        ps_oT[:, gt % 2, 0:128], h2f[:, tt * 128:(tt + 1) * 128],
                        self.ident_s[:, :]), waits=tpw)
                    evw = [o_tp]
                    if dc == 0 and (tq - 1, tt) in ydones_q:
                        evw.append(ydones_q[(tq - 1, tt)])
                    hist_oev.append(add("vector", lambda v, tt=tt, dc=dc, gt=gt: v.tensor_copy(
                        out_tm[:, tt, dc * 128:(dc + 1) * 128], ps_oT[:, gt % 2, 0:128]),
                        waits=evw))

            def emit_ln4(tq):
                x2rl = {}
                for tt in range(4):
                    deps = [hist_oev[tq * 64 + dc * 4 + tt] for dc in range(16)] + prev_oy
                    x2rl[tt] = add("sync", lambda e, tq=tq, tt=tt: e.dma_start(
                        x2r[:, :],
                        self.x2sc[tq * 512 + tt * 128:tq * 512 + (tt + 1) * 128, :]),
                        waits=[ydones[-1]] if ydones else [], dma="x2r")
                    def apply_ln4(mcol, rcol, w, tt=tt, tq=tq):
                        o_n = add("vector", lambda v, tt=tt: v.tensor_scalar(
                            sqt[:, :], out_tm[:, tt, :], mcol, rcol,
                            ALU.subtract, ALU.mult), waits=w)
                        return add("vector", lambda v, tt=tt: v.tensor_add(
                            out_tm[:, tt, :], x2r[:, :], sqt[:, :]), waits=[x2rl[tt]])
                    o_y = self._ln_tok(add, out_tm[:, tt, :], apply_ln4, sqt, mrc, deps)
                    prev_oy[:] = [o_y]
                    yop = add("sync", lambda e, tq=tq, tt=tt: e.dma_start(
                        self.yd[tq * 512 + tt * 128:tq * 512 + (tt + 1) * 128, :],
                        out_tm[:, tt, :]), waits=[o_y], dma="yst")
                    ydones.append(yop)
                    ydones_q[(tq, tt)] = yop

            for tq in range(4):
                o_l3 [tq] = add("sync", lambda e, tq=tq: e.dma_start(
                    ln3q[:, :, :],
                    self.ln3sc.ap().rearrange("(m p) j -> p m j", p=P)[
                        :, :, tq * 512:(tq + 1) * 512]),
                    waits=[q_last_mm1.get(tq - 1)], dma="l3q")
                for _ in range(2):
                    if jobs:
                        jobs.pop(0)()
                for fc in range(NFC):
                    gfc_g = tq * NFC + fc
                    fw = 128 if fc < NFC - 1 else FC_LAST
                    qs = []
                    for wi, (wqsb, wsc) in enumerate(((w1q, self.wq1sc), (w3q, self.wq3sc))):
                        gfc = gfc_g * 2 + wi
                        o_ld = add("sync", lambda e, wqsb=wqsb, wsc=wsc, fc=fc: e.dma_start(
                            wqsb[:, fc % 2, :], wsc[fc]),
                            waits=[prev_q.get(gfc - 4), mm1_hist.get((gfc_g - 2, wi))],
                            dma="wl%d_%d" % (wi, gfc_g % 2))
                        prev_q[gfc] = o_ld
                        qs.append(o_ld)
                    mmo = []
                    for wi, (wqsb, psd) in enumerate(((w1q, ps_h1), (w3q, ps_h3))):
                        def mmh(t, wqsb=wqsb, psd=psd, fc=fc, fw=fw, gfc_g=gfc_g, tq=tq):
                            for m in range(16):
                                ins = t.matmul(
                                    psd[0:fw, gfc_g % 2, :],
                                    wqsb[:, fc % 2, :].rearrange("p (m f) -> p m f", m=16)[:, :, 0:fw][:, m, :],
                                    ln3q[:, m, :],
                                    start=(m == 0), stop=(m == 15))
                            return ins
                        ws = [qs[wi], o_l3[tq], glast.get(gfc_g - 2)]
                        o_mmx = add("tensor", mmh, waits=ws)
                        mm1_hist[(gfc_g, wi)] = o_mmx
                        mmo.append(o_mmx)
                    q_last_mm1[tq] = mmo[-1]
                    o_si = add("scalar", lambda e, gfc_g=gfc_g, fw=fw: e.activation(
                        silu_t[0:fw, :], ps_h1[0:fw, gfc_g % 2, :], AF.Silu,
                        scale=self.scal[0:fw, 6:7]),
                        waits=mmo + ([glast[gfc_g - 1]] if gfc_g >= 1 else []))
                    gw = [o_si]
                    if fc == 0 and tq >= 2:
                        gw.append(mm2_hist[(tq - 2) * 16 + 15])
                    glast[gfc_g] = add("vector", lambda v, tq=tq, fc=fc, gfc_g=gfc_g, fw=fw: v.tensor_mul(
                        gsb[0:fw, tq % 2, fc, :], silu_t[0:fw, :], ps_h3[0:fw, gfc_g % 2, :]),
                        waits=gw)
                    if fc % 2 == 1 and jobs:
                        jobs.pop(0)()
                for dc in range(16):
                    jobs.append(lambda tq=tq, dc=dc: emit_mm2(tq, dc))
                jobs.append(lambda tq=tq: emit_ln4(tq))
            while jobs:
                jobs.pop(0)()
            self.emit("p56")


def host_inputs(inp, core):
    b, r = core // 4, core % 4
    x = inp["L0_attn_x"]
    out = {}
    out["xk"] = np.ascontiguousarray(x[b].T)
    out["xq"] = np.ascontiguousarray(x[b, r::4, :])
    out["xqT"] = np.ascontiguousarray(x[b, r::4, :].T)
    out["wq"] = np.ascontiguousarray(inp["L0_attn_W_q"].T)
    out["wkv"] = np.ascontiguousarray(inp["L0_attn_W_kv"].T)
    out["wo"] = np.ascontiguousarray(inp["L0_attn_W_o"].T)
    out["w1t"] = np.ascontiguousarray(inp["L0_ffn_w1"].T)
    out["w3t"] = np.ascontiguousarray(inp["L0_ffn_w3"].T)
    out["w2t"] = np.ascontiguousarray(inp["L0_ffn_w2"].T)
    inv = (np.float32(1.0) / np.float32(10000.0)) ** (
        np.arange(0, 128, 2, dtype=np.float32) / np.float32(128.0))
    t = np.arange(T, dtype=np.float32)
    fr = t[:, None] * inv[None, :]
    out["cosb"] = np.cos(fr).astype(np.float32)
    out["sinb"] = np.sin(fr).astype(np.float32)
    import ml_dtypes
    kl = np.arange(16)[:, None, None]
    i = np.arange(128)[None, :, None]
    j = np.arange(512)[None, None, :]
    out["dmask"] = ((4 * j + r) >= (128 * kl + i)).astype(ml_dtypes.bfloat16)
    out["ident"] = np.eye(P, dtype=np.float32)
    return out


def gather_y(results):
    y = np.empty((2, T, D), np.float32)
    for core in range(8):
        b, r = core // 4, core % 4
        y[b, r::4, :] = results[core]["y"]
    return y


from concourse.bass_utils import run_bass_kernel_spmd


def kernel(**inputs):
    nc = MK().build()
    in_maps = [host_inputs(inputs, c) for c in range(8)]
    res = run_bass_kernel_spmd(nc, in_maps, core_ids=list(range(8)))
    return gather_y(res.results)
